# version 03 (final)
# baseline (speedup 1.0000x reference)
import math
from contextlib import ExitStack

import numpy as np
import concourse.bass as bass
import concourse.mybir as mybir
from concourse.bass_utils import run_bass_kernel_spmd

F32 = mybir.dt.float32
BF16 = mybir.dt.bfloat16
U8 = mybir.dt.uint8
AF = mybir.ActivationFunctionType
ALU = mybir.AluOpType
AX = mybir.AxisListType

B, T, D = 4, 4096, 1024
NCORE = 8
O = 2048
KD = D // 128


class Sem:
    __slots__ = ("name", "n", "h")

    def __init__(self, name):
        self.name, self.n, self.h = name, 0, None


class Bld:
    ENGINES = ("sync", "tensor", "vector", "scalar", "gpsimd")

    def __init__(self, nc):
        self.nc = nc
        self.queues = {e: [] for e in self.ENGINES}
        self.sems = []
        for nm in ("LD", "WL", "SQ", "MM", "VE", "GP", "ST"):
            s = Sem(nm)
            self.sems.append(s)
            setattr(self, nm, s)

    def sem(self, name):
        s = Sem(name)
        self.sems.append(s)
        return s

    def op(self, eng, meth, *args, wait=(), inc=None, **kw):
        waits = [(s, s.n if v is None else v) for (s, v) in wait]
        if inc is not None:
            inc[0].n += inc[1]

        def run(e, waits=waits, inc=inc, meth=meth, args=args, kw=kw):
            for s, v in waits:
                if v > 0:
                    e.wait_ge(s.h, v)
            ins = getattr(e, meth)(*args, **kw)
            if inc is not None:
                ins.then_inc(inc[0].h, inc[1])
        self.queues[eng].append(run)

    def wait(self, eng, *waits):
        waits = [(s, s.n if v is None else v) for (s, v) in waits]

        def run(e, waits=waits):
            for s, v in waits:
                if v > 0:
                    e.wait_ge(s.h, v)
        self.queues[eng].append(run)

    def dma(self, eng, out, in_, sem=None, wait=(), **kw):
        if sem is None:
            sem = self.LD if eng == "sync" else (self.ST if "accum_op" in kw else self.WL)
        self.op(eng, "dma_start", wait=wait, inc=(sem, 16), out=out, in_=in_, **kw)

    def mm(self, out, lhsT, rhs, start, stop, wait=(), inc=None):
        self.op("tensor", "matmul", wait=wait, inc=inc, out=out, lhsT=lhsT,
                rhs=rhs, start=start, stop=stop)

    def tt(self, op, out, in0, in1, wait=(), inc=None, eng="vector"):
        self.op(eng, "tensor_tensor", out, in0, in1, op, wait=wait, inc=inc)

    def ts(self, out, in0, s1, s2, op0, op1=ALU.bypass, wait=(), inc=None,
           accum_out=None, eng="vector"):
        self.op(eng, "tensor_scalar", out, in0, s1, s2, op0, op1, wait=wait,
                inc=inc, accum_out=accum_out)

    def stt(self, out, in0, scalar, in1, op0, op1, wait=(), inc=None, eng="vector"):
        self.op(eng, "scalar_tensor_tensor", out, in0, scalar, in1, op0, op1,
                wait=wait, inc=inc)

    def act(self, out, in_, func, wait=(), inc=None, **kw):
        self.op("scalar", "activation", out, in_, func, wait=wait, inc=inc, **kw)

    def finalize(self, stack):
        for s in self.sems:
            s.h = stack.enter_context(self.nc.semaphore(s.name))
        block = stack.enter_context(self.nc.Block())
        for eng in self.ENGINES:
            ops = self.queues[eng]
            if not ops:
                continue

            def body(e, ops=ops):
                for f in ops:
                    f(e)
            getattr(block, eng)(body)


RB_MEAN, RB_RSTD = 0, 512
BC_MEAN, BC_RSTD = 0, 512


def emit_ln(b, src, dst, W, t, *, mean, eps, gam=None, bet=None, nk=KD,
            sq2=None):
    rows, rowsb, bc = t["rows"], t["rowsb"], t["bc"]
    if sq2 is None:
        sq2 = t.get("sq2", t["sq"])
    assert not (mean and sq2 is t["sq"]), "mean LN needs distinct sq2"
    ndiv = 1.0 / (nk * 128)
    mean_r = rowsb[0:1, RB_MEAN:RB_MEAN + W]
    rstd_r = rowsb[0:1, RB_RSTD:RB_RSTD + W]
    if mean:
        for k in range(nk):
            b.mm(rows[0:1, 0:W], t["ones_col"][:], src(k),
                 start=(k == 0), stop=(k == nk - 1),
                 wait=([(b.VE, None), (b.SQ, None)] if k == 0 else ()),
                 inc=((b.MM, 1) if k == nk - 1 else None))
        b.op("vector", "tensor_scalar_mul", mean_r, rows[0:1, 0:W], ndiv,
             wait=[(b.MM, b.MM.n)], inc=(b.VE, 1))
        b.mm(bc[:, BC_MEAN:BC_MEAN + W], t["ones1r"][:], mean_r, start=True,
             stop=True, wait=[(b.VE, b.VE.n)], inc=(b.MM, 1))
        mm_mbc = b.MM.n
        for k in range(nk):
            b.tt(ALU.subtract, t["sq"][:, k * W:(k + 1) * W], src(k),
                 bc[:, BC_MEAN:BC_MEAN + W],
                 wait=([(b.MM, mm_mbc)] if k == 0 else ()),
                 inc=((b.VE, 1) if k == nk - 1 else None))
        xm = lambda k: t["sq"][:, k * W:(k + 1) * W]
        ve_xm = b.VE.n
        sq_w = [(b.VE, ve_xm)]
    else:
        xm = src
        sq_w = [(b.MM, None), (b.VE, None)]
    for k in range(nk):
        b.act(sq2[:, k * W:(k + 1) * W], xm(k), AF.Square,
              wait=(sq_w if k == 0 else ()),
              inc=((b.SQ, 1) if k == nk - 1 else None))
    sq_done = b.SQ.n
    for k in range(nk):
        b.mm(rows[0:1, W:2 * W], t["ones_col"][:], sq2[:, k * W:(k + 1) * W],
             start=(k == 0), stop=(k == nk - 1),
             wait=([(b.SQ, sq_done)] if k == 0 else ()),
             inc=((b.MM, 1) if k == nk - 1 else None))
    b.ts(rstd_r, rows[0:1, W:2 * W], ndiv, eps, ALU.mult, ALU.add,
         wait=[(b.MM, b.MM.n)], inc=(b.VE, 1))
    b.act(rstd_r, rstd_r, AF.Ln, wait=[(b.VE, b.VE.n)])
    b.act(rstd_r, rstd_r, AF.Exp, scale=-0.5, inc=(b.SQ, 1))
    b.mm(bc[:, BC_RSTD:BC_RSTD + W], t["ones1r"][:], rstd_r, start=True,
         stop=True, wait=[(b.SQ, b.SQ.n)], inc=(b.MM, 1))
    mm_bc = b.MM.n
    for k in range(nk):
        w0 = [(b.MM, mm_bc)] if k == 0 else ()
        if gam is not None:
            b.stt(dst(k), xm(k), gam(k), bc[:, BC_RSTD:BC_RSTD + W],
                  ALU.mult, ALU.mult, wait=w0)
        else:
            b.tt(ALU.mult, dst(k), xm(k), bc[:, BC_RSTD:BC_RSTD + W], wait=w0)
        if bet is not None:
            b.ts(dst(k), dst(k), bet(k), None, ALU.add)
    b.op("vector", "engine_nop", inc=(b.VE, 1))


def vping(b):
    b.op("scalar", "nop", wait=[(b.VE, None)], inc=(b.SQ, 1))
    b.wait("vector", (b.SQ, b.SQ.n))


def SB(nc, *a, **k):
    return nc.sbuf_tensor(*a, allow_name_mangling=True, **k)


def PS(nc, *a, **k):
    return nc.psum_tensor(*a, allow_name_mangling=True, **k)


def rearr_in(ap, width=128):
    return ap.rearrange("(k p) t -> p k t", p=width)


def fmv(tile, k, W):
    return tile[:, k * W:(k + 1) * W]


def build_l0a():
    nc = bass.Bass("TRN2", target_bir_lowering=False, debug=False)
    xT = nc.dram_tensor("xT", [D, O], F32, kind="ExternalInput").ap()
    inw = nc.dram_tensor("inw", [128, KD], F32, kind="ExternalInput").ap()
    zT = nc.dram_tensor("zT", [D, O], F32, kind="ExternalOutput").ap()
    nT = nc.dram_tensor("nT", [D, O], F32, kind="ExternalOutput").ap()
    b = Bld(nc)
    emit_l0a(nc, b, xT, inw, zT, nT)
    with ExitStack() as stack:
        b.finalize(stack)
    return nc


def emit_l0a(nc, b, xT, inw, zT, nT):
    W = 512
    NTI = xT.shape[1] // W
    with ExitStack() as stack:
        en = stack.enter_context
        t = {}
        t["ones_col"] = en(SB(nc, "ones_col", [128, 1], F32))
        t["ones1r"] = en(SB(nc, "ones1r", [1, 128], F32))
        t["sq"] = en(SB(nc, "sq", [128, KD * W], F32))
        t["rows"] = en(PS(nc, "rows", [1, 1024], F32))
        t["rowsb"] = en(SB(nc, "rowsb", [1, 1024], F32))
        t["bc"] = en(PS(nc, "bc", [128, 1024], F32))
        vec = en(SB(nc, "vec", [128, KD], F32))
        xc = en(SB(nc, "xc", [128, 2 * KD * W], F32))
        zc = en(SB(nc, "zc", [128, KD * W], F32))
        nc_t = en(SB(nc, "nc_t", [128, KD * W], F32))

        b.op("vector", "memset", t["ones_col"][:], 1.0)
        b.op("vector", "memset", t["ones1r"][:], 1.0, inc=(b.VE, 1))
        b.dma("sync", vec[:], inw)
        for ti in range(NTI):
            buf = ti % 2
            xv = xc[:, buf * KD * W:(buf + 1) * KD * W]
            b.dma("sync", xv.rearrange("p (k t) -> p k t", k=KD),
                  rearr_in(xT[:, ti * W:(ti + 1) * W]),
                  wait=[(b.VE, None)] if ti >= 2 else ())
            ld = b.LD.n
            b.wait("scalar", (b.LD, ld))
            b.wait("tensor", (b.LD, ld))
            emit_ln(b, lambda k: fmv(xv, k, W), lambda k: fmv(zc, k, W), W, t,
                    mean=True, eps=1e-5, sq2=zc)
            emit_ln(b, lambda k: fmv(zc, k, W), lambda k: fmv(nc_t, k, W), W, t,
                    mean=False, eps=1e-6, gam=lambda k: vec[:, k:k + 1])
            b.dma("gpsimd", zT[:, ti * W:(ti + 1) * W].rearrange("(k p) t -> p k t", p=128),
                  zc[:].rearrange("p (k t) -> p k t", k=KD), sem=b.ST,
                  wait=[(b.VE, None)])
            b.dma("gpsimd", nT[:, ti * W:(ti + 1) * W].rearrange("(k p) t -> p k t", p=128),
                  nc_t[:].rearrange("p (k t) -> p k t", k=KD), sem=b.ST,
                  wait=[(b.VE, None)])
            b.wait("vector", (b.ST, b.ST.n))
        b.wait("gpsimd", (b.ST, b.ST.n))


def build_l0b():
    nc = bass.Bass("TRN2", target_bir_lowering=False, debug=False)
    nprev = nc.dram_tensor("nprev", [D, O], F32, kind="ExternalInput").ap()
    nown = nc.dram_tensor("nown", [D, O], F32, kind="ExternalInput").ap()
    zT = nc.dram_tensor("zT", [D, O], F32, kind="ExternalInput").ap()
    xT = nc.dram_tensor("xT", [D, O], F32, kind="ExternalInput").ap()
    alog = nc.dram_tensor("alog", [128, KD], F32, kind="ExternalInput").ap()
    cout = nc.dram_tensor("cout", [128, KD], F32, kind="ExternalInput").ap()
    dskip = nc.dram_tensor("dskip", [128, KD], F32, kind="ExternalInput").ap()
    ffw = nc.dram_tensor("ffw", [128, KD], F32, kind="ExternalInput").ap()
    fcw = nc.dram_tensor("fcw", [D, 2048], F32, kind="ExternalInput").ap()
    pjw = nc.dram_tensor("pjw", [2048, D], F32, kind="ExternalInput").ap()
    xo = nc.dram_tensor("xo", [D, O], F32, kind="ExternalOutput").ap()
    b = Bld(nc)
    emit_l0b(nc, b, nprev, nown, zT, xT, alog, cout, dskip, ffw, fcw, pjw, xo,
             zero_prefix_carry=False)
    with ExitStack() as stack:
        b.finalize(stack)
    return nc


def emit_l0b(nc, b, nprev, nown, zT, xT, alog, cout, dskip, ffw, fcw, pjw, xo,
             zero_prefix_carry):
    W = 512
    NTI = O // W
    with ExitStack() as stack:
        en = stack.enter_context
        t = {}
        t["ones_col"] = en(SB(nc, "ones_col", [128, 1], F32))
        t["ones1r"] = en(SB(nc, "ones1r", [1, 128], F32))
        t["sq"] = en(SB(nc, "sq", [128, KD * W], F32))
        t["rows"] = en(PS(nc, "rows", [1, 1024], F32))
        t["rowsb"] = en(SB(nc, "rowsb", [1, 1024], F32))
        t["bc"] = en(PS(nc, "bc", [128, 1024], F32))
        vec = en(SB(nc, "vec", [128, 4 * KD], F32))
        carry = en(SB(nc, "carry", [128, KD], F32))
        x1 = en(SB(nc, "x1", [128, KD * O], F32))
        fcwb = en(SB(nc, "fcwb", [128, KD * 2048], BF16))
        pjwb = en(SB(nc, "pjwb", [128, 16 * D], BF16))
        nob = en(SB(nc, "nob", [128, KD * W], F32))
        zcb = en(SB(nc, "zcb", [128, KD * W], F32))
        n2t = en(SB(nc, "n2t", [128, KD * W], BF16))
        hid = en(SB(nc, "hid", [128, 16 * W], BF16))
        xot = zcb
        ffp = en(PS(nc, "ffp", [128, 1024], F32))

        b.op("vector", "memset", t["ones_col"][:], 1.0)
        b.op("vector", "memset", t["ones1r"][:], 1.0)
        b.op("vector", "memset", carry[:], 0.0, inc=(b.VE, 1))
        b.dma("sync", vec[:, 0:KD], alog)
        b.dma("sync", vec[:, KD:2 * KD], cout)
        b.dma("sync", vec[:, 2 * KD:3 * KD], dskip)
        b.dma("sync", vec[:, 3 * KD:4 * KD], ffw)
        ld0 = b.LD.n
        b.dma("gpsimd", fcwb[:].rearrange("p (k h) -> p k h", k=KD),
              fcw.rearrange("(k p) h -> p k h", p=128))
        b.dma("gpsimd", pjwb[:].rearrange("p (k h) -> p k h", k=16),
              pjw.rearrange("(k p) h -> p k h", p=128))
        wl0 = b.WL.n
        b.act(vec[:, 0:KD], vec[:, 0:KD], AF.Exp, wait=[(b.LD, ld0)])
        b.act(vec[:, 0:KD], vec[:, 0:KD], AF.Exp, scale=-1.0, inc=(b.SQ, 1))
        a_done = b.SQ.n
        a_col = lambda k: vec[:, k:k + 1]
        c_col = lambda k: vec[:, KD + k:KD + k + 1]
        d_col = lambda k: vec[:, 2 * KD + k:2 * KD + k + 1]
        f_col = lambda k: vec[:, 3 * KD + k:3 * KD + k + 1]

        carry_view = lambda buf: buf[:].rearrange("p (k w) -> p k w", k=KD)[:, :, W - 1:W].squeeze(2)
        for ti in range(NTI):
            c0 = ti * W
            b.dma("sync", nob[:].rearrange("p (k t) -> p k t", k=KD),
                  rearr_in(nprev[:, c0:c0 + W]),
                  wait=([(b.SQ, None)] if ti > 0 else ()))
            ldp = b.LD.n
            b.wait("vector", (b.LD, ldp), (b.SQ, a_done))
            if ti > 0:
                b.wait("vector", (b.SQ, b.SQ.n))
            for k in range(KD):
                b.op("vector", "tensor_tensor_scan", fmv(nob, k, W),
                     a_col(k).broadcast_to([128, W]), fmv(nob, k, W),
                     (0.0 if ti == 0 else carry[:, k:k + 1]), ALU.mult, ALU.add,
                     inc=((b.VE, 1) if k == KD - 1 else None))
            b.op("scalar", "copy", carry[:], carry_view(nob),
                 wait=[(b.VE, b.VE.n)], inc=(b.SQ, 1))
        if zero_prefix_carry:
            b.op("scalar", "mul", carry[:], carry[:], 0.0, inc=(b.SQ, 1))

        for ti in range(NTI):
            c0 = ti * W
            b.dma("sync", nob[:].rearrange("p (k t) -> p k t", k=KD),
                  rearr_in(nown[:, c0:c0 + W]), wait=[(b.VE, None)])
            b.dma("sync", zcb[:].rearrange("p (k t) -> p k t", k=KD),
                  rearr_in(zT[:, c0:c0 + W]))
            ld = b.LD.n
            b.wait("vector", (b.LD, ld), (b.SQ, b.SQ.n))
            for k in range(KD):
                b.op("vector", "tensor_tensor_scan", fmv(t["sq"], k, W),
                     a_col(k).broadcast_to([128, W]), fmv(nob, k, W),
                     carry[:, k:k + 1], ALU.mult, ALU.add,
                     inc=((b.VE, 1) if k == KD - 1 else None))
            b.op("scalar", "copy", carry[:], carry_view(t["sq"]),
                 wait=[(b.VE, b.VE.n)], inc=(b.SQ, 1))
            b.wait("vector", (b.SQ, b.SQ.n))
            for k in range(KD):
                b.op("vector", "tensor_scalar_mul", fmv(nob, k, W),
                     fmv(nob, k, W), d_col(k))
                b.stt(fmv(nob, k, W), fmv(t["sq"], k, W), c_col(k), fmv(nob, k, W),
                      ALU.mult, ALU.add)
                b.tt(ALU.add, x1[:, k * O + c0:k * O + c0 + W], fmv(zcb, k, W),
                     fmv(nob, k, W))
            b.op("vector", "engine_nop", inc=(b.VE, 1))

        b.wait("tensor", (b.WL, wl0))
        for ti in range(NTI):
            c0 = ti * W
            x1s = lambda k, a=0, w=W: x1[:, k * O + c0 + a:k * O + c0 + a + w]
            emit_ln(b, x1s, lambda k: fmv(n2t, k, W), W, t, mean=False, eps=1e-6,
                    gam=f_col)
            ve_n2 = b.VE.n
            sq_n2 = b.SQ.n
            for m in range(16):
                pb = ffp[:, (m % 2) * 512:(m % 2) * 512 + 512]
                for k in range(KD):
                    b.mm(pb, fcwb[:, k * 2048 + m * 128:k * 2048 + (m + 1) * 128],
                         fmv(n2t, k, W), start=(k == 0), stop=(k == KD - 1),
                         wait=([(b.VE, ve_n2), (b.SQ, max(sq_n2, b.SQ.n))] if k == 0 else ()),
                         inc=((b.MM, 1) if k == KD - 1 else None))
                b.act(hid[:, m * W:(m + 1) * W], pb, AF.Silu,
                      wait=[(b.MM, b.MM.n)], inc=(b.SQ, 1))
            b.dma("sync", xot[:].rearrange("p (k t) -> p k t", k=KD),
                  rearr_in(xT[:, c0:c0 + W]),
                  wait=[(b.ST, b.ST.n), (b.VE, None)])
            ldx = b.LD.n
            sq_hid = b.SQ.n
            for o in range(KD):
                pb = ffp[:, (o % 2) * 512:(o % 2) * 512 + 512]
                for k in range(16):
                    b.mm(pb, pjwb[:, k * D + o * 128:k * D + (o + 1) * 128],
                         hid[:, k * W:(k + 1) * W], start=(k == 0), stop=(k == 15),
                         wait=([(b.SQ, sq_hid), (b.VE, None)] if k == 0 else ()),
                         inc=((b.MM, 1) if k == 15 else None))
                mm_o = b.MM.n
                b.tt(ALU.add, fmv(xot, o, W), fmv(xot, o, W), x1s(o),
                     wait=[(b.MM, mm_o), (b.LD, ldx)])
                b.tt(ALU.add, fmv(xot, o, W), fmv(xot, o, W), pb, inc=(b.VE, 1))
            b.dma("gpsimd", xo[:, c0:c0 + W].rearrange("(k p) t -> p k t", p=128),
                  xot[:].rearrange("p (k t) -> p k t", k=KD), sem=b.ST,
                  wait=[(b.VE, None)])
        b.wait("gpsimd", (b.ST, b.ST.n))


def build_l3ffn():
    nc = bass.Bass("TRN2", target_bir_lowering=False, debug=False)
    xT = nc.dram_tensor("xT", [D, O], F32, kind="ExternalInput").ap()
    x_tm = nc.dram_tensor("x_tm", [O, D], F32, kind="ExternalInput").ap()
    gw = nc.dram_tensor("gw", [D, 8], F32, kind="ExternalInput").ap()
    identh = nc.dram_tensor("identh", [128, 128], BF16, kind="ExternalInput").ap()
    w1s = nc.dram_tensor("w1s", [9, D, 2048], F32, kind="ExternalInput").ap()
    w3s = nc.dram_tensor("w3s", [9, D, 2048], F32, kind="ExternalInput").ap()
    w2s = nc.dram_tensor("w2s", [9, 2048, D], F32, kind="ExternalInput").ap()
    xo = nc.dram_tensor("xo", [O, D], F32, kind="ExternalOutput").ap()
    b = Bld(nc)
    emit_l3ffn(nc, b, xT, x_tm, gw, identh, w1s, w3s, w2s, xo)
    with ExitStack() as stack:
        b.finalize(stack)
    return nc


def emit_l3ffn(nc, b, xT, x_tm, gw, identh, w1s, w3s, w2s, xo):
    WL_ = 256
    NSUB = O // 128
    NH = 2
    HT = O // NH
    with ExitStack() as stack:
        en = stack.enter_context
        t = {}
        t["ones_col"] = en(SB(nc, "ones_col", [128, 1], F32))
        t["ones1r"] = en(SB(nc, "ones1r", [1, 128], F32))

        t["rows"] = en(PS(nc, "rows", [1, 512], F32))
        t["rowsb"] = en(SB(nc, "rowsb", [1, 1024], F32))
        t["bc"] = en(PS(nc, "bc", [128, 1024], F32))

        zb = en(SB(nc, "zb", [128, KD * O], BF16))
        yacc = en(SB(nc, "yacc", [128, NSUB * D], F32))
        ht = en(SB(nc, "ht", [128, 16 * HT], BF16))
        ht32 = ht[:].bitcast(F32)
        xc = ht32[:, 0:KD * WL_]
        t["sq"] = ht32[:, KD * WL_:2 * KD * WL_]
        t["sq2"] = ht32[:, 2 * KD * WL_:3 * KD * WL_]
        w2b = en(SB(nc, "w2b", [128, 8 * D], BF16))
        w1m = en(SB(nc, "w1m", [128, 2 * KD * 128], BF16))
        w3m = en(SB(nc, "w3m", [128, 2 * KD * 128], BF16))
        gwb = en(SB(nc, "gwb", [128, KD * 8], BF16))
        zsb = en(SB(nc, "zsb", [128, KD * O], BF16))
        ident = en(SB(nc, "ident", [128, 128], BF16))
        lgtb = en(SB(nc, "lgtb", [128, O], BF16))
        ce = en(SB(nc, "ce", [128, NSUB * 9], F32))
        gat = en(SB(nc, "gat", [128, 160], F32))
        onesb1r = en(SB(nc, "onesb1r", [1, 128], BF16))
        srow = en(SB(nc, "srow", [1, O], BF16))
        identf3 = en(SB(nc, "identf3", [128, 128], F32))

        gs = en(SB(nc, "gs", [128, 512], F32))
        gps = en(PS(nc, "gps", [128, 1024], F32))
        ffp = en(PS(nc, "ffp", [128, 1024], F32))
        tps = en(PS(nc, "tps", [128, 8], BF16))
        tpc3 = t["bc"]

        b.op("vector", "memset", t["ones_col"][:], 1.0)
        b.op("vector", "memset", onesb1r[:], 1.0)
        b.op("vector", "memset", t["ones1r"][:], 1.0, inc=(b.VE, 1))
        b.dma("sync", ident[:], identh)
        ld_g = b.LD.n
        b.op("vector", "tensor_copy", identf3[:], ident[:],
             wait=[(b.LD, ld_g)], inc=(b.VE, 1))
        b.dma("gpsimd", gwb[:].rearrange("p (k e) -> p k e", k=KD),
              gw.rearrange("(k p) e -> p k e", p=128))
        wl_g = b.WL.n
        b.dma("gpsimd", yacc[:].rearrange("p (s d) -> p s d", s=NSUB),
              x_tm.rearrange("(s p) d -> p s d", p=128))
        wl_x = b.WL.n

        for ti in range(O // WL_):
            b.dma("sync", xc.rearrange("p (k t) -> p k t", k=KD),
                  rearr_in(xT[:, ti * WL_:(ti + 1) * WL_]),
                  wait=[(b.VE, None)])
            ld = b.LD.n
            b.wait("scalar", (b.LD, ld))
            b.wait("tensor", (b.LD, ld))
            emit_ln(b, lambda k: xc[:, k * WL_:(k + 1) * WL_],
                    lambda k: zb[:, k * O + ti * WL_:k * O + ti * WL_ + WL_],
                    WL_, t, mean=True, eps=1e-5)

        b.wait("tensor", (b.WL, wl_g))
        for ti in range(4):
            for k in range(KD):
                b.mm(gps[0:8, 0:512], gwb[:, k * 8:(k + 1) * 8],
                     zb[:, k * O + ti * 512:k * O + (ti + 1) * 512],
                     start=(k == 0), stop=(k == KD - 1),
                     wait=([(b.VE, None)] if k == 0 else ()),
                     inc=((b.MM, 1) if k == KD - 1 else None))
            b.op("vector", "tensor_copy", lgtb[0:8, ti * 512:(ti + 1) * 512],
                 gps[0:8, 0:512], wait=[(b.MM, b.MM.n)], inc=(b.VE, 1))
        E0, MXg, SC = 0, 128, 144
        e0 = lambda s: gat[:, E0 + 8 * s:E0 + 8 * (s + 1)]
        for s in range(NSUB):
            b.op("tensor", "transpose", tps[:, 0:8],
                 lgtb[0:8, s * 128:(s + 1) * 128], ident[0:8, 0:8],
                 wait=[(b.VE, None), (b.LD, ld_g)], inc=(b.MM, 1))
            b.op("vector", "tensor_copy", e0(s), tps[:, 0:8],
                 wait=[(b.MM, b.MM.n)], inc=(b.VE, 1))
        for s in range(NSUB):
            b.op("vector", "reduce_max", gat[:, MXg + s:MXg + s + 1], e0(s), AX.X,
                 inc=((b.VE, 1) if s == NSUB - 1 else None))
        for s in range(NSUB):
            b.ts(ce[:, s * 9:s * 9 + 8], e0(s), gat[:, MXg + s:MXg + s + 1],
                 None, ALU.is_ge)
        for s in range(NSUB):
            b.op("vector", "memset", ce[:, s * 9 + 8:s * 9 + 9], 1.0,
                 inc=((b.VE, 1) if s == NSUB - 1 else None))
        for s in range(NSUB):
            b.act(gat[:, SC + s:SC + s + 1], gat[:, MXg + s:MXg + s + 1],
                  AF.Sigmoid, wait=([(b.VE, None)] if s == 0 else ()),
                  inc=((b.SQ, 1) if s == NSUB - 1 else None))
        b.wait("vector", (b.SQ, b.SQ.n))
        for s in range(NSUB):
            b.op("tensor", "transpose", tpc3[0:1, 0:128],
                 gat[:, SC + s:SC + s + 1], identf3[:],
                 wait=[(b.VE, None), (b.SQ, None)], inc=(b.MM, 1))
            b.op("scalar", "copy", srow[0:1, s * 128:(s + 1) * 128],
                 tpc3[0:1, 0:128], wait=[(b.MM, b.MM.n)], inc=(b.SQ, 1))
        for ti in range(4):
            b.mm(gps[:, 0:512], onesb1r[:],
                 srow[0:1, ti * 512:(ti + 1) * 512], start=True, stop=True,
                 wait=[(b.SQ, None), (b.VE, None)], inc=(b.MM, 1))
            mm_sc = b.MM.n
            for k in range(KD):
                b.tt(ALU.mult, zsb[:, k * O + ti * 512:k * O + (ti + 1) * 512],
                     zb[:, k * O + ti * 512:k * O + (ti + 1) * 512],
                     gps[:, 0:512],
                     wait=([(b.MM, mm_sc)] if k == 0 else ()),
                     inc=((b.VE, 1) if k == KD - 1 else None))

        b.wait("tensor", (b.WL, wl_x))
        for e in range(9):
            for half in range(NH):
                tok0 = half * HT
                for m in range(16):
                    buf = m % 2
                    w1v = w1m[:, buf * KD * 128:(buf + 1) * KD * 128]
                    w3v = w3m[:, buf * KD * 128:(buf + 1) * KD * 128]
                    b.dma("gpsimd", w1v.rearrange("p (k c) -> p k c", k=KD),
                          w1s[e, :, m * 128:(m + 1) * 128].rearrange("(k p) c -> p k c", p=128),
                          wait=[(b.MM, None)] if (e + half + m) else ())
                    b.dma("gpsimd", w3v.rearrange("p (k c) -> p k c", k=KD),
                          w3s[e, :, m * 128:(m + 1) * 128].rearrange("(k p) c -> p k c", p=128))
                    wl_m = b.WL.n
                    for ti in range(HT // 512):
                        tk = tok0 + ti * 512
                        zin = zb if e == 8 else zsb
                        for k in range(KD):
                            b.mm(gps[:, 0:512], w1v[:, k * 128:(k + 1) * 128],
                                 zin[:, k * O + tk:k * O + tk + 512],
                                 start=(k == 0), stop=(k == KD - 1),
                                 wait=([(b.WL, wl_m), (b.SQ, None), (b.VE, None)]
                                       if k == 0 else ()))
                        for k in range(KD):
                            b.mm(gps[:, 512:1024], w3v[:, k * 128:(k + 1) * 128],
                                 zin[:, k * O + tk:k * O + tk + 512],
                                 start=(k == 0), stop=(k == KD - 1),
                                 inc=((b.MM, 1) if k == KD - 1 else None))
                        b.act(gs[:], gps[:, 0:512], AF.Silu,
                              wait=[(b.MM, b.MM.n)], inc=(b.SQ, 1))
                        b.tt(ALU.mult, ht[:, m * HT + ti * 512:m * HT + (ti + 1) * 512],
                             gs[:], gps[:, 512:1024], wait=[(b.SQ, None)],
                             inc=(b.VE, 1))
                for kh in range(2):
                    b.dma("gpsimd", w2b[:].rearrange("p (k d) -> p k d", k=8),
                          w2s[e, kh * 1024:(kh + 1) * 1024, :].rearrange("(k p) d -> p k d", p=128),
                          wait=[(b.MM, None)])
                    wl_2 = b.WL.n
                    for s in range(8):
                        sg = half * 8 + s
                        for n in range(2):
                            pb = ffp[:, (n % 2) * 512:(n % 2) * 512 + 512]
                            for k in range(8):
                                b.mm(pb, ht[:, (kh * 8 + k) * HT + s * 128:(kh * 8 + k) * HT + (s + 1) * 128],
                                     w2b[:, k * D + n * 512:k * D + (n + 1) * 512],
                                     start=(k == 0), stop=(k == 7),
                                     wait=([(b.WL, wl_2), (b.VE, None)] if k == 0 else ()),
                                     inc=((b.MM, 1) if k == 7 else None))
                            b.stt(yacc[:, sg * D + n * 512:sg * D + (n + 1) * 512],
                                  pb, ce[:, sg * 9 + e:sg * 9 + e + 1],
                                  yacc[:, sg * D + n * 512:sg * D + (n + 1) * 512],
                                  ALU.mult, ALU.add, wait=[(b.MM, b.MM.n)],
                                  inc=(b.VE, 1))
        b.dma("gpsimd", xo.rearrange("(s p) d -> p s d", p=128),
              yacc[:].rearrange("p (s d) -> p s d", s=NSUB), sem=b.ST,
              wait=[(b.VE, None)])
        b.wait("gpsimd", (b.ST, b.ST.n))


def build_l0ffn():
    nc = bass.Bass("TRN2", target_bir_lowering=False, debug=False)
    xT = nc.dram_tensor("xT", [D, O], F32, kind="ExternalInput").ap()
    x_tm = nc.dram_tensor("x_tm", [O, D], F32, kind="ExternalInput").ap()
    gw = nc.dram_tensor("gw", [D, 8], F32, kind="ExternalInput").ap()
    gbc = nc.dram_tensor("gbc", [128, 8], F32, kind="ExternalInput").ap()
    identh = nc.dram_tensor("identh", [128, 128], BF16, kind="ExternalInput").ap()
    w1s = nc.dram_tensor("w1s", [9, D, 2048], F32, kind="ExternalInput").ap()
    w3s = nc.dram_tensor("w3s", [9, D, 2048], F32, kind="ExternalInput").ap()
    w2s = nc.dram_tensor("w2s", [9, 2048, D], F32, kind="ExternalInput").ap()
    xo = nc.dram_tensor("xo", [O, D], F32, kind="ExternalOutput").ap()
    b = Bld(nc)
    emit_l0ffn(nc, b, xT, x_tm, gw, gbc, identh, w1s, w3s, w2s, xo)
    with ExitStack() as stack:
        b.finalize(stack)
    return nc


def emit_l0ffn(nc, b, xT, x_tm, gw, gbc, identh, w1s, w3s, w2s, xo):
    WL_ = 256
    NSUB = O // 128
    NH = 2
    HT = O // NH
    with ExitStack() as stack:
        en = stack.enter_context
        t = {}
        t["ones_col"] = en(SB(nc, "ones_col", [128, 1], F32))
        t["ones1r"] = en(SB(nc, "ones1r", [1, 128], F32))

        t["rows"] = en(PS(nc, "rows", [1, 512], F32))
        t["rowsb"] = en(SB(nc, "rowsb", [1, 1024], F32))
        t["bc"] = en(PS(nc, "bc", [128, 1024], F32))

        zb = en(SB(nc, "zb", [128, KD * O], BF16))
        yacc = en(SB(nc, "yacc", [128, NSUB * D], F32))
        ht = en(SB(nc, "ht", [128, 16 * HT], BF16))
        ht32 = ht[:].bitcast(F32)
        xc = ht32[:, 0:KD * WL_]
        t["sq"] = ht32[:, KD * WL_:2 * KD * WL_]
        t["sq2"] = ht32[:, 2 * KD * WL_:3 * KD * WL_]
        w2b = en(SB(nc, "w2b", [128, 8 * D], BF16))
        w1m = en(SB(nc, "w1m", [128, 2 * KD * 128], BF16))
        w3m = en(SB(nc, "w3m", [128, 2 * KD * 128], BF16))
        gwb = en(SB(nc, "gwb", [128, KD * 8], BF16))
        gbt = en(SB(nc, "gbt", [128, 8], F32))
        ident = en(SB(nc, "ident", [128, 128], BF16))
        lgtb = en(SB(nc, "lgtb", [128, O], BF16))
        ce = en(SB(nc, "ce", [128, NSUB * 9], F32))
        gat = en(SB(nc, "gat", [128, 848], F32))
        onesb1r = en(SB(nc, "onesb1r", [1, 128], BF16))
        srow = en(SB(nc, "srow", [1, O], BF16))
        identf3 = en(SB(nc, "identf3", [128, 128], F32))

        gs = en(SB(nc, "gs", [128, 512], F32))
        gps = en(PS(nc, "gps", [128, 1024], F32))
        ffp = en(PS(nc, "ffp", [128, 1024], F32))
        tps = en(PS(nc, "tps", [128, 8], BF16))

        b.op("vector", "memset", t["ones_col"][:], 1.0)
        b.op("vector", "memset", onesb1r[:], 1.0)
        b.op("vector", "memset", t["ones1r"][:], 1.0, inc=(b.VE, 1))
        b.dma("sync", gbt[:], gbc)
        b.dma("sync", ident[:], identh)
        ld_g = b.LD.n
        b.dma("gpsimd", gwb[:].rearrange("p (k e) -> p k e", k=KD),
              gw.rearrange("(k p) e -> p k e", p=128))
        wl_g = b.WL.n
        b.dma("gpsimd", yacc[:].rearrange("p (s d) -> p s d", s=NSUB),
              x_tm.rearrange("(s p) d -> p s d", p=128))
        wl_x = b.WL.n

        for ti in range(O // WL_):
            b.dma("sync", xc.rearrange("p (k t) -> p k t", k=KD),
                  rearr_in(xT[:, ti * WL_:(ti + 1) * WL_]),
                  wait=[(b.VE, None)])
            ld = b.LD.n
            b.wait("scalar", (b.LD, ld))
            b.wait("tensor", (b.LD, ld))
            emit_ln(b, lambda k: xc[:, k * WL_:(k + 1) * WL_],
                    lambda k: zb[:, k * O + ti * WL_:k * O + ti * WL_ + WL_],
                    WL_, t, mean=True, eps=1e-5)

        b.wait("tensor", (b.WL, wl_g))
        for ti in range(4):
            for k in range(KD):
                b.mm(gps[0:8, 0:512], gwb[:, k * 8:(k + 1) * 8],
                     zb[:, k * O + ti * 512:k * O + (ti + 1) * 512],
                     start=(k == 0), stop=(k == KD - 1),
                     wait=([(b.VE, None)] if k == 0 else ()),
                     inc=((b.MM, 1) if k == KD - 1 else None))
            b.op("vector", "tensor_copy", lgtb[0:8, ti * 512:(ti + 1) * 512],
                 gps[0:8, 0:512], wait=[(b.MM, b.MM.n)], inc=(b.VE, 1))
        E0, BS, EQ1, EQ2, CES = 0, 128, 256, 384, 512
        MXg, NMX, ZS, REC, MX2, GBZ = 640, 656, 672, 688, 704, 720
        e0 = lambda s: gat[:, E0 + 8 * s:E0 + 8 * (s + 1)]
        for s in range(NSUB):
            b.op("tensor", "transpose", tps[:, 0:8],
                 lgtb[0:8, s * 128:(s + 1) * 128], ident[0:8, 0:8],
                 wait=[(b.VE, None), (b.LD, ld_g)], inc=(b.MM, 1))
            b.op("vector", "tensor_copy", e0(s), tps[:, 0:8],
                 wait=[(b.MM, b.MM.n)], inc=(b.VE, 1))
        for s in range(NSUB):
            b.op("vector", "reduce_max", gat[:, MXg + s:MXg + s + 1], e0(s), AX.X)
        for s in range(NSUB):
            b.op("vector", "tensor_scalar_mul", gat[:, NMX + s:NMX + s + 1],
                 gat[:, MXg + s:MXg + s + 1], -1.0,
                 inc=((b.VE, 1) if s == NSUB - 1 else None))
        for s in range(NSUB):
            b.act(e0(s), e0(s), AF.Exp, bias=gat[:, NMX + s:NMX + s + 1],
                  accum_out=gat[:, ZS + s:ZS + s + 1],
                  wait=([(b.VE, None)] if s == 0 else ()),
                  inc=((b.SQ, 1) if s == NSUB - 1 else None))
        sq_exp = b.SQ.n
        b.op("vector", "reciprocal", gat[:, REC:REC + NSUB],
             gat[:, ZS:ZS + NSUB], wait=[(b.SQ, sq_exp)])
        for s in range(NSUB):
            b.ts(gat[:, GBZ + 8 * s:GBZ + 8 * (s + 1)], gbt[:],
                 gat[:, ZS + s:ZS + s + 1], None, ALU.mult,
                 inc=((b.VE, 1) if s == NSUB - 1 else None))
        vping(b)
        b.tt(ALU.add, gat[:, BS:BS + 128], gat[:, E0:E0 + 128],
             gat[:, GBZ:GBZ + 128], inc=(b.VE, 1))
        vping(b)
        for s in range(NSUB):
            b.op("vector", "reduce_max", gat[:, MXg + s:MXg + s + 1],
                 gat[:, BS + 8 * s:BS + 8 * (s + 1)], AX.X)
        for s in range(NSUB):
            b.ts(gat[:, EQ1 + 8 * s:EQ1 + 8 * (s + 1)],
                 gat[:, BS + 8 * s:BS + 8 * (s + 1)],
                 gat[:, MXg + s:MXg + s + 1], None, ALU.is_ge,
                 inc=((b.VE, 1) if s == NSUB - 1 else None))
        vping(b)
        b.stt(gat[:, EQ2:EQ2 + 128], gat[:, EQ1:EQ1 + 128], -1e30,
              gat[:, BS:BS + 128], ALU.mult, ALU.add, inc=(b.VE, 1))
        vping(b)
        for s in range(NSUB):
            b.op("vector", "reduce_max", gat[:, MX2 + s:MX2 + s + 1],
                 gat[:, EQ2 + 8 * s:EQ2 + 8 * (s + 1)], AX.X)
        for s in range(NSUB):
            b.ts(gat[:, EQ2 + 8 * s:EQ2 + 8 * (s + 1)],
                 gat[:, EQ2 + 8 * s:EQ2 + 8 * (s + 1)],
                 gat[:, MX2 + s:MX2 + s + 1], None, ALU.is_ge,
                 inc=((b.VE, 1) if s == NSUB - 1 else None))
        vping(b)
        b.tt(ALU.add, gat[:, EQ1:EQ1 + 128], gat[:, EQ1:EQ1 + 128],
             gat[:, EQ2:EQ2 + 128], inc=(b.VE, 1))
        vping(b)
        b.tt(ALU.mult, gat[:, CES:CES + 128], gat[:, E0:E0 + 128],
             gat[:, EQ1:EQ1 + 128], inc=(b.VE, 1))
        vping(b)
        for s in range(NSUB):
            b.ts(ce[:, s * 9:s * 9 + 8], gat[:, CES + 8 * s:CES + 8 * (s + 1)],
                 gat[:, REC + s:REC + s + 1], None, ALU.mult)
        for s in range(NSUB):
            b.op("vector", "memset", ce[:, s * 9 + 8:s * 9 + 9], 1.0,
                 inc=((b.VE, 1) if s == NSUB - 1 else None))

        b.wait("tensor", (b.WL, wl_x))
        for e in range(9):
            for half in range(NH):
                tok0 = half * HT
                for m in range(16):
                    buf = m % 2
                    w1v = w1m[:, buf * KD * 128:(buf + 1) * KD * 128]
                    w3v = w3m[:, buf * KD * 128:(buf + 1) * KD * 128]
                    b.dma("gpsimd", w1v.rearrange("p (k c) -> p k c", k=KD),
                          w1s[e, :, m * 128:(m + 1) * 128].rearrange("(k p) c -> p k c", p=128),
                          wait=[(b.MM, None)] if (e + half + m) else ())
                    b.dma("gpsimd", w3v.rearrange("p (k c) -> p k c", k=KD),
                          w3s[e, :, m * 128:(m + 1) * 128].rearrange("(k p) c -> p k c", p=128))
                    wl_m = b.WL.n
                    for ti in range(HT // 512):
                        tk = tok0 + ti * 512
                        for k in range(KD):
                            b.mm(gps[:, 0:512], w1v[:, k * 128:(k + 1) * 128],
                                 zb[:, k * O + tk:k * O + tk + 512],
                                 start=(k == 0), stop=(k == KD - 1),
                                 wait=([(b.WL, wl_m), (b.SQ, None), (b.VE, None)]
                                       if k == 0 else ()))
                        for k in range(KD):
                            b.mm(gps[:, 512:1024], w3v[:, k * 128:(k + 1) * 128],
                                 zb[:, k * O + tk:k * O + tk + 512],
                                 start=(k == 0), stop=(k == KD - 1),
                                 inc=((b.MM, 1) if k == KD - 1 else None))
                        b.act(gs[:], gps[:, 0:512], AF.Silu,
                              wait=[(b.MM, b.MM.n)], inc=(b.SQ, 1))
                        b.tt(ALU.mult, ht[:, m * HT + ti * 512:m * HT + (ti + 1) * 512],
                             gs[:], gps[:, 512:1024], wait=[(b.SQ, None)],
                             inc=(b.VE, 1))
                for kh in range(2):
                    b.dma("gpsimd", w2b[:].rearrange("p (k d) -> p k d", k=8),
                          w2s[e, kh * 1024:(kh + 1) * 1024, :].rearrange("(k p) d -> p k d", p=128),
                          wait=[(b.MM, None)])
                    wl_2 = b.WL.n
                    for s in range(8):
                        sg = half * 8 + s
                        for n in range(2):
                            pb = ffp[:, (n % 2) * 512:(n % 2) * 512 + 512]
                            for k in range(8):
                                b.mm(pb, ht[:, (kh * 8 + k) * HT + s * 128:(kh * 8 + k) * HT + (s + 1) * 128],
                                     w2b[:, k * D + n * 512:k * D + (n + 1) * 512],
                                     start=(k == 0), stop=(k == 7),
                                     wait=([(b.WL, wl_2), (b.VE, None)] if k == 0 else ()),
                                     inc=((b.MM, 1) if k == 7 else None))
                            b.stt(yacc[:, sg * D + n * 512:sg * D + (n + 1) * 512],
                                  pb, ce[:, sg * 9 + e:sg * 9 + e + 1],
                                  yacc[:, sg * D + n * 512:sg * D + (n + 1) * 512],
                                  ALU.mult, ALU.add, wait=[(b.MM, b.MM.n)],
                                  inc=(b.VE, 1))
        b.dma("gpsimd", xo.rearrange("(s p) d -> p s d", p=128),
              yacc[:].rearrange("p (s d) -> p s d", s=NSUB), sem=b.ST,
              wait=[(b.VE, None)])
        b.wait("gpsimd", (b.ST, b.ST.n))


def stage_l0ffn(inp, xs):
    gw = np.ascontiguousarray(inp["L0_ffn_gate_w"].T)
    gbc = np.ascontiguousarray(np.tile(inp["L0_ffn_gate_bias"][None, :], (128, 1)))
    import ml_dtypes
    identh = np.eye(128, dtype=ml_dtypes.bfloat16)
    w1s = np.concatenate([inp["L0_ffn_w1"], inp["L0_ffn_sw1"][None]], 0)
    w1s = np.ascontiguousarray(w1s.transpose(0, 2, 1))
    w3s = np.concatenate([inp["L0_ffn_w3"], inp["L0_ffn_sw3"][None]], 0)
    w3s = np.ascontiguousarray(w3s.transpose(0, 2, 1))
    w2s = np.concatenate([inp["L0_ffn_w2"], inp["L0_ffn_sw2"][None]], 0)
    w2s = np.ascontiguousarray(w2s.transpose(0, 2, 1))
    maps = []
    for c in range(NCORE):
        bb, h = c // 2, c % 2
        xonw = xs[bb, h * O:(h + 1) * O]
        maps.append({
            "xT": np.ascontiguousarray(xonw.T), "x_tm": np.ascontiguousarray(xonw),
            "gw": gw, "gbc": gbc, "identh": identh,
            "w1s": w1s, "w3s": w3s, "w2s": w2s,
        })
    return maps


def l0_ffn(inp, xs):
    maps = stage_l0ffn(inp, xs)
    r = run("l0ffn", build_l0ffn, maps)
    out = np.empty_like(xs)
    for c in range(NCORE):
        bb, h = c // 2, c % 2
        out[bb, h * O:(h + 1) * O] = r[c]["xo"]
    return out


def build_l1attn():
    nc = bass.Bass("TRN2", target_bir_lowering=False, debug=False)
    S = T
    io = {}
    io["xkv"] = nc.dram_tensor("xkv", [D, S], F32, kind="ExternalInput").ap()
    io["x_tm"] = nc.dram_tensor("x_tm", [S, D], F32, kind="ExternalInput").ap()
    for nm, shp in (("gamh", [128, KD]), ("beth", [128, KD]), ("wq", [128, 128]),
                    ("wk", [128, 128]), ("wv", [128, 128]), ("wpt", [128, 128]),
                    ("bq", [128, 1]), ("bk", [128, 1]), ("bvr", [1, 128]),
                    ("ub", [128, 8]), ("vb", [128, 8]), ("wo", [D, D]),
                    ("bobc", [128, D]),
                    ("cosq", [8, 64, S]), ("sinq", [8, 64, S]),
                    ("cos2q", [8, 64, S]), ("sin2q", [8, 64, S]),
                    ("philo", [8, 128, S]), ("phiup", [8, 128, S]),
                    ("mloh", [4, 128, 512]), ("muph", [4, 128, 512]),
                    ("mcr1h", [128, 512])):
        io[nm] = nc.dram_tensor(nm, shp, F32, kind="ExternalInput").ap()
    io["identh"] = nc.dram_tensor("identh", [128, 128], BF16, kind="ExternalInput").ap()
    io["zscr"] = nc.dram_tensor("zscr", [D, S], BF16, kind="Internal").ap()
    io["cscr"] = nc.dram_tensor("cscr", [8, 128, S], BF16, kind="Internal").ap()
    io["xo"] = nc.dram_tensor("xo", [S, D], F32, kind="ExternalOutput").ap()
    b = Bld(nc)
    emit_l1attn(nc, b, io)
    with ExitStack() as stack:
        b.finalize(stack)
    return nc


def emit_l1attn(nc, b, io):
    S = T
    (xkv, x_tm, gamh, beth, wq, wk, wv, wpt, bq, bk, bvr, ub, vb, wo, bobc,
     cosq, sinq, cos2q, sin2q, philo, phiup, mloh, muph, mcr1h, identh, zscr,
     cscr, xo) = (io[k] for k in (
        "xkv", "x_tm", "gamh", "beth", "wq", "wk", "wv", "wpt", "bq", "bk",
        "bvr", "ub", "vb", "wo", "bobc", "cosq", "sinq", "cos2q", "sin2q",
        "philo", "phiup", "mloh", "muph", "mcr1h", "identh", "zscr", "cscr",
        "xo"))
    WL_ = 256
    QG = 2
    NQT = O // 128
    with ExitStack() as stack:
        en = stack.enter_context
        t = {}
        t["ones_col"] = en(SB(nc, "ones_col", [128, 1], F32))
        onesb = en(SB(nc, "onesb", [128, 1], BF16))
        t["ones1r"] = en(SB(nc, "ones1r", [1, 128], F32))
        t["rows"] = en(PS(nc, "rows", [1, 512], F32))
        t["rowsb"] = en(SB(nc, "rowsb", [1, 1024], F32))
        t["bc"] = en(PS(nc, "bc", [128, 1024], F32))
        pbuf = en(SB(nc, "pbuf", [128, QG * S], BF16))
        ptb = en(SB(nc, "ptb", [128, 32 * 128 * QG], BF16))
        pb32 = pbuf[:].bitcast(F32)
        xc = lambda: pb32[:, 0:KD * WL_]
        zc = lambda: pb32[:, KD * WL_:2 * KD * WL_]
        t["sq"] = ptb[:].bitcast(F32)[:, 0:KD * WL_]
        sq2v = ptb[:].bitcast(F32)[:, KD * WL_:2 * KD * WL_]
        gbt = en(SB(nc, "gbt", [128, 2 * KD], F32))
        znt = en(SB(nc, "znt", [128, KD * WL_], BF16))
        wqt = en(SB(nc, "wqt", [128, 4 * 128], BF16))
        colv = en(SB(nc, "colv", [128, 24], F32))
        bvb = en(SB(nc, "bvb", [128, 128], F32))
        zh = en(SB(nc, "zh", [128, S], BF16))
        kbf = en(SB(nc, "kbf", [128, S], BF16))
        vtk = en(SB(nc, "vtk", [128, 32 * 128], BF16))
        qu = en(SB(nc, "qu", [128, O], BF16))
        qv = en(SB(nc, "qv", [128, O + 512], BF16))
        ufe = en(SB(nc, "ufe", [64, O + 512], BF16))
        ufo = en(SB(nc, "ufo", [64, O + 512], BF16))
        ab = en(SB(nc, "ab", [128, 2 * O], BF16))
        tcs = en(SB(nc, "tcs", [64, 4 * O], BF16))
        phl = en(SB(nc, "phl", [128, S], BF16))
        phu = en(SB(nc, "phu", [128, S], BF16))
        mlt = en(SB(nc, "mlt", [128, 4 * 512], BF16))
        mut = en(SB(nc, "mut", [128, 4 * 512], BF16))
        mc1 = en(SB(nc, "mc1", [128, 512], BF16))
        ssb = en(SB(nc, "ssb", [128, 2 * 512], F32))
        bndr = en(SB(nc, "bndr", [1, 3 * O], BF16))
        k8 = en(SB(nc, "k8", [1, 16], F32))
        bcol = en(SB(nc, "bcol", [128, 16], F32))
        zrec = en(SB(nc, "zrec", [1, QG * 128], F32))
        ident = en(SB(nc, "ident", [128, 128], BF16))
        ctxh = en(SB(nc, "ctxh", [128, O], BF16))
        ctxs = en(SB(nc, "ctxs", [128, 2 * KD * 128], BF16))
        wob = en(SB(nc, "wob", [128, KD * D], BF16))
        bob = en(SB(nc, "bob", [128, D], F32))
        xtm = en(SB(nc, "xtm", [128, 2 * D], F32))
        outb = en(SB(nc, "outb", [128, 2 * D], F32))
        pss = en(PS(nc, "pss", [128, 3 * 512], F32))
        ptp = en(PS(nc, "ptp", [128, 512], BF16))
        cpv = en(PS(nc, "cpv", [128, 512], F32))

        b.op("vector", "memset", t["ones_col"][:], 1.0)
        b.op("vector", "memset", onesb[:], 1.0)
        b.op("vector", "memset", t["ones1r"][:], 1.0, inc=(b.VE, 1))
        b.dma("sync", colv[:, 0:1], bq)
        b.dma("sync", colv[:, 1:2], bk)
        b.dma("sync", colv[:, 2:10], ub)
        b.dma("sync", colv[:, 10:18], vb)
        b.dma("sync", ident[:], identh)
        b.dma("sync", bob[:], bobc)
        b.dma("sync", gbt[:, 0:KD], gamh)
        b.dma("sync", gbt[:, KD:2 * KD], beth)
        b.dma("sync", t["rowsb"][0:1, 0:128], bvr)
        ld0 = b.LD.n
        b.dma("gpsimd", wqt[:, 0:128], wq)
        b.dma("gpsimd", wqt[:, 128:256], wk)
        b.dma("gpsimd", wqt[:, 256:384], wv)
        b.dma("gpsimd", wqt[:, 384:448], wpt[:, 0:64])
        b.dma("gpsimd", wqt[:, 448:512], wpt[:, 64:128])
        b.dma("gpsimd", mlt[:].rearrange("p (f c) -> p f c", f=4),
              mloh.rearrange("f p c -> p f c"))
        b.dma("gpsimd", mut[:].rearrange("p (f c) -> p f c", f=4),
              muph.rearrange("f p c -> p f c"))
        b.dma("gpsimd", mc1[:], mcr1h)
        b.dma("gpsimd", wob[:].rearrange("p (k d) -> p k d", k=KD),
              wo.rearrange("(k p) d -> p k d", p=128))
        wl0 = b.WL.n
        b.mm(t["bc"][:, 0:128], t["ones1r"][:], t["rowsb"][0:1, 0:128],
             start=True, stop=True, wait=[(b.LD, ld0), (b.VE, None)],
             inc=(b.MM, 1))
        b.op("vector", "tensor_copy", bvb[:], t["bc"][:, 0:128],
             wait=[(b.MM, b.MM.n)], inc=(b.VE, 1))

        for ti in range(S // WL_):
            b.dma("sync", xc().rearrange("p (k t) -> p k t", k=KD),
                  rearr_in(xkv[:, ti * WL_:(ti + 1) * WL_]), wait=[(b.VE, None)])
            ld = b.LD.n
            b.wait("scalar", (b.LD, ld))
            b.wait("tensor", (b.LD, ld))
            emit_ln(b, lambda k: fmv(xc(), k, WL_), lambda k: fmv(zc(), k, WL_),
                    WL_, t, mean=True, eps=1e-5, sq2=sq2v)
            emit_ln(b, lambda k: fmv(zc(), k, WL_), lambda k: fmv(znt, k, WL_),
                    WL_, t, mean=True, eps=1e-5, sq2=sq2v,
                    gam=lambda k: gbt[:, k:k + 1],
                    bet=lambda k: gbt[:, KD + k:KD + k + 1])
            b.dma("gpsimd",
                  zscr[:, ti * WL_:(ti + 1) * WL_].rearrange("(k p) t -> p k t", p=128),
                  znt[:].rearrange("p (k t) -> p k t", k=KD), sem=b.ST,
                  wait=[(b.VE, None)])
            b.wait("vector", (b.ST, b.ST.n))
        st_zn = b.ST.n

        for h in range(8):
            b.dma("sync", zh[:], zscr[h * 128:(h + 1) * 128, :],
                  wait=[(b.ST, st_zn), (b.MM, None)])
            ld_zh = b.LD.n
            b.dma("gpsimd", phl[:], philo[h], wait=[(b.MM, None), (b.VE, None)])
            b.dma("gpsimd", phu[:], phiup[h])
            wl_h0 = b.WL.n
            b.tt(ALU.add, colv[:, 20:21], colv[:, 0:1], colv[:, 2 + h:3 + h],
                 wait=[(b.SQ, None)])
            b.tt(ALU.add, colv[:, 21:22], colv[:, 0:1], colv[:, 10 + h:11 + h],
                 inc=(b.VE, 1))
            ve_bias = b.VE.n
            for kt in range(8):
                pb = pss[:, (kt % 2) * 512:(kt % 2) * 512 + 512]
                b.mm(pb, wqt[:, 128:256], zh[:, kt * 512:(kt + 1) * 512],
                     start=True, stop=True,
                     wait=[(b.LD, ld_zh), (b.WL, wl0), (b.SQ, None)],
                     inc=(b.MM, 1))
                b.op("scalar", "add", kbf[:, kt * 512:(kt + 1) * 512], pb,
                     colv[:, 1:2], wait=[(b.MM, b.MM.n), (b.VE, ve_bias)],
                     inc=(b.SQ, 1))
            for s in range(32):
                pb = pss[:, (s % 2) * 512:(s % 2) * 512 + 128]
                b.mm(pb, zh[:, s * 128:(s + 1) * 128], wqt[:, 256:384],
                     start=True, stop=True,
                     wait=[(b.VE, None), (b.SQ, None)], inc=(b.MM, 1))
                b.tt(ALU.add, vtk[:, s * 128:(s + 1) * 128], pb, bvb[:],
                     wait=[(b.MM, b.MM.n)], inc=(b.VE, 1))
            for q_base in (0, O):
                qb2 = q_base // 512
                b.dma("gpsimd", tcs[:, 0:O], cosq[h][:, q_base:q_base + O],
                      wait=[(b.VE, None), (b.SQ, None)])
                b.dma("gpsimd", tcs[:, O:2 * O], sinq[h][:, q_base:q_base + O])
                b.dma("gpsimd", tcs[:, 2 * O:3 * O], cos2q[h][:, q_base:q_base + O])
                b.dma("gpsimd", tcs[:, 3 * O:4 * O], sin2q[h][:, q_base:q_base + O])
                wl_h = b.WL.n
                nqv = 5 if q_base == 0 else 4
                for qt in range(5):
                    if qt >= nqv:
                        continue
                    pb = pss[:, (qt % 2) * 512:(qt % 2) * 512 + 512]
                    b.mm(pb, wqt[:, 0:128],
                         zh[:, q_base + qt * 512:q_base + (qt + 1) * 512],
                         start=True, stop=True, wait=[(b.SQ, None)], inc=(b.MM, 1))
                    mm_q = b.MM.n
                    if qt < 4:
                        b.op("scalar", "add", qu[:, qt * 512:(qt + 1) * 512], pb,
                             colv[:, 20:21], wait=[(b.MM, mm_q)])
                    b.op("scalar", "add", qv[:, qt * 512:(qt + 1) * 512], pb,
                         colv[:, 21:22],
                         wait=([(b.MM, mm_q)] if qt == 4 else ()), inc=(b.SQ, 1))
                sq_q = b.SQ.n
                for ti in range(nqv):
                    pb = pss[:, (ti % 2) * 512:(ti % 2) * 512 + 512]
                    b.mm(pb[0:64, :], wqt[:, 384:448], qv[:, ti * 512:(ti + 1) * 512],
                         start=True, stop=True,
                         wait=[(b.SQ, None), (b.VE, None)])
                    b.mm(pb[64:128, :], wqt[:, 448:512], qv[:, ti * 512:(ti + 1) * 512],
                         start=True, stop=True, inc=(b.MM, 1))
                    mm_u = b.MM.n
                    b.op("vector", "tensor_copy", ufe[:, ti * 512:(ti + 1) * 512],
                         pb[0:64, :], wait=[(b.MM, mm_u)], inc=(b.VE, 1))
                    b.op("scalar", "copy", ufo[:, ti * 512:(ti + 1) * 512],
                         pb[64:128, :],
                         wait=[(b.MM, mm_u)] + ([(b.VE, None)] if ti == 0 else []),
                         inc=(b.SQ, 1))
                b.op("vector", "engine_nop", inc=(b.VE, 1))
                ue = lambda off=0: ufe[:, off:O + off]
                uo = lambda off=0: ufo[:, off:O + off]
                cq = tcs[:, 0:O]
                sq_ = tcs[:, O:2 * O]
                c2 = tcs[:, 2 * O:3 * O]
                s2 = tcs[:, 3 * O:4 * O]
                t1 = pbuf[:].bitcast(F32)[0:64, 0:O]
                t2 = pbuf[:].bitcast(F32)[0:64, O:2 * O]
                b.wait("vector", (b.WL, wl_h), (b.SQ, b.SQ.n))
                b.tt(ALU.mult, t1, ue(), cq)
                b.tt(ALU.mult, t2, uo(), sq_)
                b.tt(ALU.add, ab[0:64, 0:O], t1, t2)
                b.tt(ALU.mult, t1, uo(), cq)
                b.tt(ALU.mult, t2, ue(), sq_)
                b.tt(ALU.subtract, ab[64:128, 0:O], t1, t2)
                b.tt(ALU.mult, t1, ue(1), c2)
                b.tt(ALU.mult, t2, uo(1), s2)
                b.tt(ALU.add, ab[0:64, O:2 * O], t1, t2)
                b.tt(ALU.mult, t1, uo(1), c2)
                b.tt(ALU.mult, t2, ue(1), s2)
                b.tt(ALU.subtract, ab[64:128, O:2 * O], t1, t2, inc=(b.VE, 1))
                for kt in range(8):
                    b.act(pbuf[:, kt * 512:(kt + 1) * 512],
                          kbf[:, kt * 512:(kt + 1) * 512], AF.Square,
                          wait=([(b.VE, None), (b.MM, None)] if kt == 0 else ()),
                          inc=((b.SQ, 1) if kt == 7 else None))
                for kt in range(8):
                    b.mm(t["rows"][0:1, 0:512], onesb[:],
                         pbuf[:, kt * 512:(kt + 1) * 512], start=True, stop=True,
                         wait=([(b.SQ, None)] if kt == 0 else [(b.VE, None)]),
                         inc=(b.MM, 1))
                    b.op("vector", "reduce_max", k8[0:1, kt:kt + 1],
                         t["rows"][0:1, 0:512], AX.X, wait=[(b.MM, b.MM.n)],
                         inc=(b.VE, 1))
                for (idx, srcb) in ((0, qu[:, 0:O]), (1, ab[:, 0:O]),
                                    (2, ab[:, O:2 * O])):
                    b.act(pbuf[:, 0:O], srcb, AF.Square, wait=[(b.VE, None)],
                          inc=(b.SQ, 1))
                    sqd = b.SQ.n
                    for j in range(4):
                        b.mm(t["rows"][0:1, 0:512], onesb[:],
                             pbuf[:, j * 512:(j + 1) * 512], start=True,
                             stop=True, wait=[(b.SQ, None)], inc=(b.MM, 1))
                        b.act(bndr[0:1, idx * O + j * 512:idx * O + (j + 1) * 512],
                              t["rows"][0:1, 0:512], AF.Sqrt,
                              wait=[(b.MM, b.MM.n)], inc=(b.SQ, 1))
                sq_nrm = b.SQ.n
                b.op("vector", "reduce_max", k8[0:1, 8:9], k8[0:1, 0:8], AX.X,
                     wait=[(b.SQ, sq_nrm)], inc=(b.VE, 1))
                b.act(k8[0:1, 9:10], k8[0:1, 8:9], AF.Sqrt, wait=[(b.VE, b.VE.n)],
                      inc=(b.SQ, 1))
                b.tt(ALU.add, bndr[0:1, O:2 * O], bndr[0:1, O:2 * O],
                     bndr[0:1, 2 * O:3 * O], wait=[(b.SQ, b.SQ.n)])
                b.ts(bndr[0:1, O:2 * O], bndr[0:1, O:2 * O], 8.1, None, ALU.mult)
                b.ts(bndr[0:1, 0:O], bndr[0:1, 0:O], k8[0:1, 9:10], 1.02,
                     ALU.mult, ALU.mult)
                b.tt(ALU.add, bndr[0:1, 0:O], bndr[0:1, 0:O], bndr[0:1, O:2 * O])
                b.ts(bndr[0:1, 0:O], bndr[0:1, 0:O], -1.0 / 32.0, None, ALU.mult,
                     inc=(b.VE, 1))
                ve_bnd = b.VE.n
                for qt in range(16):
                    b.op("tensor", "transpose", ptp[:, 2 * qt:2 * qt + 1],
                         bndr[0:1, qt * 128:(qt + 1) * 128], ident[0:1, 0:1],
                         wait=([(b.VE, ve_bnd), (b.SQ, None)] if qt == 0 else ()),
                         inc=(b.MM, 1))
                mm_bias = b.MM.n
                b.op("scalar", "copy", bcol[:], ptp[:, 0:32:2],
                     wait=[(b.MM, mm_bias)], inc=(b.SQ, 1))
                sq_bias = b.SQ.n
                for qg in range(NQT // QG):
                    for qi in range(QG):
                        qt = qg * QG + qi
                        ktd, ph_ = qb2 + qt // 4, qt % 4
                        for kt in range(8):
                            par = (qt * 8 + kt) % 2
                            pA = pss[:, par * 512:par * 512 + 512]
                            quv = qu[:, qt * 128:(qt + 1) * 128]
                            diag = (kt == ktd)
                            corner = (kt == ktd + 1 and ph_ == 3)
                            if not diag and not corner:
                                if kt < ktd:
                                    posL = ab[:, 0 * O + qt * 128:0 * O + (qt + 1) * 128]
                                else:
                                    posL = ab[:, O + qt * 128:O + (qt + 1) * 128]
                                posR = (phl if kt < ktd else phu)[:, kt * 512:(kt + 1) * 512]
                                b.mm(pA, quv, kbf[:, kt * 512:(kt + 1) * 512],
                                     start=True, stop=False,
                                     wait=[(b.SQ, None), (b.VE, None)])
                                b.mm(pA, posL, posR, start=False, stop=True,
                                     inc=(b.MM, 1))
                                b.act(pbuf[:, qi * S + kt * 512:qi * S + (kt + 1) * 512],
                                      pA, AF.Exp, scale=1.0 / 32.0,
                                      bias=bcol[:, qt:qt + 1],
                                      wait=[(b.MM, b.MM.n), (b.SQ, sq_bias)],
                                      inc=(b.SQ, 1))
                            else:
                                pB = pss[:, 1024:1536]
                                b.mm(pA, quv, kbf[:, kt * 512:(kt + 1) * 512],
                                     start=True, stop=True,
                                     wait=[(b.SQ, None), (b.VE, None)])
                                if diag:
                                    b.mm(pB[:, 0:256], ab[:, qt * 128:(qt + 1) * 128],
                                         phl[:, kt * 512:kt * 512 + 256],
                                         start=True, stop=True)
                                    b.mm(pB[:, 256:512], ab[:, qt * 128:(qt + 1) * 128],
                                         phl[:, kt * 512 + 256:(kt + 1) * 512],
                                         start=True, stop=True, inc=(b.MM, 1))
                                    mmB = b.MM.n
                                    b.tt(ALU.mult, ssb[:, 0:512], pB,
                                         mlt[:, ph_ * 512:(ph_ + 1) * 512],
                                         wait=[(b.MM, mmB), (b.SQ, None)],
                                         inc=(b.VE, 1))
                                    b.mm(pB[:, 0:256], ab[:, O + qt * 128:O + (qt + 1) * 128],
                                         phu[:, kt * 512:kt * 512 + 256],
                                         start=True, stop=True,
                                         wait=[(b.VE, b.VE.n)])
                                    b.mm(pB[:, 256:512], ab[:, O + qt * 128:O + (qt + 1) * 128],
                                         phu[:, kt * 512 + 256:(kt + 1) * 512],
                                         start=True, stop=True, inc=(b.MM, 1))
                                    mmB2 = b.MM.n
                                    b.tt(ALU.mult, ssb[:, 512:1024], pB,
                                         mut[:, ph_ * 512:(ph_ + 1) * 512],
                                         wait=[(b.MM, mmB2)])
                                    b.tt(ALU.add, ssb[:, 0:512], ssb[:, 0:512],
                                         ssb[:, 512:1024])
                                    b.tt(ALU.add, ssb[:, 0:512], ssb[:, 0:512],
                                         pA, inc=(b.VE, 1))
                                else:
                                    b.mm(pB[:, 0:256],
                                         ab[:, O + qt * 128:O + (qt + 1) * 128],
                                         phu[:, kt * 512:kt * 512 + 256],
                                         start=True, stop=True)
                                    b.mm(pB[:, 256:512],
                                         ab[:, O + qt * 128:O + (qt + 1) * 128],
                                         phu[:, kt * 512 + 256:(kt + 1) * 512],
                                         start=True, stop=True, inc=(b.MM, 1))
                                    mmB = b.MM.n
                                    b.tt(ALU.mult, ssb[:, 0:512], pB, mc1[:],
                                         wait=[(b.MM, mmB), (b.SQ, None)])
                                    b.tt(ALU.add, ssb[:, 0:512], ssb[:, 0:512],
                                         pA, inc=(b.VE, 1))
                                b.act(pbuf[:, qi * S + kt * 512:qi * S + (kt + 1) * 512],
                                      ssb[:, 0:512], AF.Exp, scale=1.0 / 32.0,
                                      bias=bcol[:, qt:qt + 1],
                                      wait=[(b.VE, b.VE.n), (b.SQ, sq_bias)],
                                      inc=(b.SQ, 1))
                    sq_exp = b.SQ.n
                    for ks in range(32):
                        for qi in range(QG):
                            b.op("tensor", "transpose", ptp[:, qi * 128:(qi + 1) * 128],
                                 pbuf[:, qi * S + ks * 128:qi * S + (ks + 1) * 128],
                                 ident[:],
                                 wait=([(b.SQ, None), (b.VE, None)] if qi == 0 else ()),
                                 inc=((b.MM, 1) if qi == QG - 1 else None))
                        mm_tr = b.MM.n
                        if ks % 2 == 0:
                            b.op("vector", "tensor_copy",
                                 ptb[:, ks * QG * 128:(ks + 1) * QG * 128],
                                 ptp[:, 0:QG * 128], wait=[(b.MM, mm_tr)],
                                 inc=(b.VE, 1))
                        else:
                            b.op("scalar", "copy",
                                 ptb[:, ks * QG * 128:(ks + 1) * QG * 128],
                                 ptp[:, 0:QG * 128], wait=[(b.MM, mm_tr)],
                                 inc=(b.SQ, 1))
                    ve_tb, sq_tb = b.VE.n, b.SQ.n
                    for ks in range(32):
                        b.mm(t["rows"][0:1, 0:QG * 128], onesb[:],
                             ptb[:, ks * QG * 128:(ks + 1) * QG * 128],
                             start=(ks == 0), stop=(ks == 31),
                             wait=([(b.VE, ve_tb), (b.SQ, sq_tb)] if ks == 0 else ()),
                             inc=((b.MM, 1) if ks == 31 else None))
                    b.op("vector", "reciprocal", zrec[0:1, :],
                         t["rows"][0:1, 0:QG * 128], wait=[(b.MM, b.MM.n)],
                         inc=(b.VE, 1))
                    b.mm(t["bc"][:, 0:QG * 128], t["ones1r"][:], zrec[0:1, :],
                         start=True, stop=True, wait=[(b.VE, b.VE.n)],
                         inc=(b.MM, 1))
                    mm_zbc = b.MM.n
                    for ks in range(32):
                        b.mm(cpv[:, 0:QG * 128], vtk[:, ks * 128:(ks + 1) * 128],
                             ptb[:, ks * QG * 128:(ks + 1) * QG * 128],
                             start=(ks == 0), stop=(ks == 31),
                             wait=([(b.SQ, None)] if ks == 0 else ()),
                             inc=((b.MM, 1) if ks == 31 else None))
                    b.op("scalar", "copy", ssb[:, 0:QG * 128], cpv[:, 0:QG * 128],
                         wait=[(b.MM, b.MM.n)], inc=(b.SQ, 1))
                    b.tt(ALU.mult, ctxh[:, qg * QG * 128:(qg + 1) * QG * 128],
                         ssb[:, 0:QG * 128], t["bc"][:, 0:QG * 128],
                         wait=[(b.SQ, b.SQ.n), (b.MM, mm_zbc)], inc=(b.VE, 1))
                b.dma("gpsimd", cscr[h][:, q_base:q_base + O], ctxh[:],
                      sem=b.ST, wait=[(b.VE, None)])
                b.wait("vector", (b.ST, b.ST.n))

        st_ctx = b.ST.n
        for s in range(S // 128):
            buf = s % 2
            b.dma("sync", ctxs[:, buf * 1024:(buf + 1) * 1024].rearrange("p (k c) -> p k c", k=KD),
                  cscr[:, :, s * 128:(s + 1) * 128].rearrange("k p c -> p k c"),
                  wait=[(b.ST, st_ctx), (b.MM, None)])
            b.dma("sync", xtm[:, buf * D:(buf + 1) * D],
                  x_tm[s * 128:(s + 1) * 128, :], wait=[(b.VE, None)])
            ld_s = b.LD.n
            for n in range(2):
                pb = pss[:, n * 512:n * 512 + 512]
                for hh in range(KD):
                    b.mm(pb, ctxs[:, buf * 1024 + hh * 128:buf * 1024 + (hh + 1) * 128],
                         wob[:, hh * D + n * 512:hh * D + (n + 1) * 512],
                         start=(hh == 0), stop=(hh == KD - 1),
                         wait=([(b.LD, ld_s), (b.VE, None)] if hh == 0 else ()),
                         inc=((b.MM, 1) if hh == KD - 1 else None))
                b.tt(ALU.add, outb[:, buf * D + n * 512:buf * D + (n + 1) * 512],
                     pb, bob[:, n * 512:(n + 1) * 512],
                     wait=[(b.MM, b.MM.n), (b.ST, b.ST.n)])
                b.tt(ALU.add, outb[:, buf * D + n * 512:buf * D + (n + 1) * 512],
                     outb[:, buf * D + n * 512:buf * D + (n + 1) * 512],
                     xtm[:, buf * D + n * 512:buf * D + (n + 1) * 512],
                     inc=(b.VE, 1))
            b.dma("gpsimd", xo[s * 128:(s + 1) * 128, :],
                  outb[:, buf * D:(buf + 1) * D], sem=b.ST, wait=[(b.VE, None)])
        b.wait("gpsimd", (b.ST, b.ST.n))


def stage_l1attn(inp, xs):
    import ml_dtypes
    S, H = T, 8
    om = np.exp(np.arange(0, D, 2, dtype=np.float64) * (-math.log(10000.0) / D))
    gam = _vec8(inp["L1_attn_ln_gamma"])
    bet = _vec8(inp["L1_attn_ln_beta"])
    wq = np.ascontiguousarray(inp["L1_attn_Wq"])
    wk = np.ascontiguousarray(inp["L1_attn_Wk"])
    wv = np.ascontiguousarray(inp["L1_attn_Wv"])
    perm = np.concatenate([np.arange(0, 128, 2), np.arange(1, 128, 2)])
    wpt = np.ascontiguousarray(inp["L1_attn_Wp"].T[:, perm])
    bq = np.ascontiguousarray(inp["L1_attn_bq"].reshape(128, 1))
    bk = np.ascontiguousarray(inp["L1_attn_bk"].reshape(128, 1))
    bvr = np.ascontiguousarray(inp["L1_attn_bv"].reshape(1, 128))
    ub = np.ascontiguousarray(inp["L1_attn_u_bias"].T)
    vb = np.ascontiguousarray(inp["L1_attn_v_bias"].T)
    wo = np.ascontiguousarray(inp["L1_attn_Wo"])
    bobc = np.ascontiguousarray(np.tile(inp["L1_attn_bo"][None, :], (128, 1)))
    identh = np.eye(128, dtype=ml_dtypes.bfloat16)
    ii = np.arange(128)[:, None]
    jj = np.arange(512)[None, :]
    mlo = np.stack([(jj <= ii + 128 * p) for p in range(4)]).astype(np.float32)
    mup = np.stack([(jj >= ii + 128 * p + 2) for p in range(4)]).astype(np.float32)
    mcr1 = np.ones((128, 512), np.float32)
    mcr1[127, 0] = 0.0
    kg = np.arange(S).astype(np.float64)
    philo = np.empty((H, 128, S), np.float32)
    phiup = np.empty((H, 128, S), np.float32)
    cosq = np.empty((H, 64, S), np.float32)
    sinq = np.empty((H, 64, S), np.float32)
    cos2q = np.empty((H, 64, S), np.float32)
    sin2q = np.empty((H, 64, S), np.float32)
    for hh in range(H):
        w = om[64 * hh:64 * hh + 64][:, None]
        philo[hh, 0:64] = np.sin((S - 1 + kg)[None, :] * w)
        philo[hh, 64:128] = np.cos((S - 1 + kg)[None, :] * w)
        phiup[hh, 0:64] = np.sin(kg[None, :] * w)
        phiup[hh, 64:128] = np.cos(kg[None, :] * w)
        cosq[hh] = np.cos(kg[None, :] * w)
        sinq[hh] = np.sin(kg[None, :] * w)
        cos2q[hh] = np.cos((kg + 2)[None, :] * w)
        sin2q[hh] = np.sin((kg + 2)[None, :] * w)
    shared = {
        "gamh": gam, "beth": bet, "wq": wq, "wk": wk, "wv": wv, "wpt": wpt,
        "bq": bq, "bk": bk, "bvr": bvr, "ub": ub, "vb": vb, "wo": wo,
        "bobc": bobc, "identh": identh,
        "cosq": cosq, "sinq": sinq, "cos2q": cos2q, "sin2q": sin2q,
        "philo": philo, "phiup": phiup,
        "mloh": mlo, "muph": mup, "mcr1h": mcr1,
    }
    maps = []
    for c in range(NCORE):
        bb = c // 2
        m = dict(shared)
        m["xkv"] = np.ascontiguousarray(xs[bb].T)
        m["x_tm"] = np.ascontiguousarray(xs[bb])
        maps.append(m)
    return maps


def l1_attn(inp, xs):
    maps = stage_l1attn(inp, xs)
    r = run("l1attn", build_l1attn, maps)
    out = np.empty_like(xs)
    for bb in range(B):
        out[bb] = r[2 * bb]["xo"]
    return out


def build_l1ffn():
    nc = bass.Bass("TRN2", target_bir_lowering=False, debug=False)
    xT = nc.dram_tensor("xT", [D, O], F32, kind="ExternalInput").ap()
    gw = nc.dram_tensor("gw", [D, 4], F32, kind="ExternalInput").ap()
    gbt4 = nc.dram_tensor("gbt4", [128, 64], F32, kind="ExternalInput").ap()
    identh = nc.dram_tensor("identh", [128, 128], BF16, kind="ExternalInput").ap()
    w1h = nc.dram_tensor("w1h", [4, D, D], F32, kind="ExternalInput").ap()
    w2h = nc.dram_tensor("w2h", [4, D, D], F32, kind="ExternalInput").ap()
    colh = nc.dram_tensor("colh", [128, 4 * 6 * KD], F32, kind="ExternalInput").ap()
    xo = nc.dram_tensor("xo", [D, O], F32, kind="ExternalOutput").ap()
    b = Bld(nc)
    emit_l1ffn(nc, b, xT, gw, gbt4, identh, w1h, w2h, colh, xo)
    with ExitStack() as stack:
        b.finalize(stack)
    return nc


def emit_l1ffn(nc, b, xT, gw, gbt4, identh, w1h, w2h, colh, xo):
    W = 256
    NTI = O // W
    NSUB = O // 128
    with ExitStack() as stack:
        en = stack.enter_context
        t = {}
        t["ones_col"] = en(SB(nc, "ones_col", [128, 1], F32))
        t["ones1r"] = en(SB(nc, "ones1r", [1, 128], F32))
        t["rows"] = en(PS(nc, "rows", [1, 1024], F32))
        t["rowsb"] = en(SB(nc, "rowsb", [1, 1024], F32))
        t["bc"] = en(PS(nc, "bc", [128, 1024], F32))
        t["sq"] = en(SB(nc, "sq", [128, KD * W], F32))
        t["sq2"] = en(SB(nc, "sq2", [128, KD * W], F32))
        yaccF = en(SB(nc, "yaccF", [128, KD * O], F32))
        zb = en(SB(nc, "zb", [128, KD * O], BF16))
        w1b = en(SB(nc, "w1b", [128, KD * D], BF16))
        w2b = en(SB(nc, "w2b", [128, KD * D], BF16))
        t1f = en(SB(nc, "t1f", [128, KD * W], F32))
        hb = en(SB(nc, "hb", [128, KD * W], BF16))
        gwb = en(SB(nc, "gwb", [128, KD * 4], F32))
        gbt = en(SB(nc, "gbt", [128, 64], F32))
        colt = en(SB(nc, "colt", [128, 4 * 6 * KD], F32))
        ident = en(SB(nc, "ident", [128, 128], BF16))
        identf = en(SB(nc, "identf", [128, 128], F32))
        lgtb = en(SB(nc, "lgtb", [128, O], F32))
        gat = en(SB(nc, "gat", [128, 512], F32))
        crow = en(SB(nc, "crow", [1, 4 * O], BF16))
        onesb1r = en(SB(nc, "onesb1r", [1, 128], BF16))
        xtt = t1f
        gps = en(PS(nc, "gps", [128, 1024], F32))
        tps = en(PS(nc, "tps", [128, 8], F32))
        tpc = en(PS(nc, "tpc", [1, 128], F32))

        b.op("vector", "memset", t["ones_col"][:], 1.0)
        b.op("vector", "memset", onesb1r[:], 1.0)
        b.op("vector", "memset", t["ones1r"][:], 1.0, inc=(b.VE, 1))
        b.dma("sync", gbt[:], gbt4)
        b.dma("sync", ident[:], identh)
        b.dma("sync", colt[:], colh)
        ld0 = b.LD.n
        b.dma("sync", gwb[:].rearrange("p (k e) -> p k e", k=KD),
              gw.rearrange("(k p) e -> p k e", p=128))
        wl0 = b.WL.n
        b.op("vector", "tensor_copy", identf[:], ident[:],
             wait=[(b.LD, ld0)], inc=(b.VE, 1))
        ccol = lambda e, p_, k: colt[:, e * 48 + p_ * 8 + k:e * 48 + p_ * 8 + k + 1]

        for ti in range(NTI):
            b.dma("sync", xtt[:].rearrange("p (k t) -> p k t", k=KD),
                  rearr_in(xT[:, ti * W:(ti + 1) * W]), wait=[(b.VE, None)])
            ld = b.LD.n
            b.wait("scalar", (b.LD, ld))
            b.wait("tensor", (b.LD, ld))
            emit_ln(b, lambda k: fmv(xtt, k, W),
                    lambda k: yaccF[:, k * O + ti * W:k * O + ti * W + W],
                    W, t, mean=True, eps=1e-5)
            for k in range(KD):
                b.op("vector", "tensor_copy",
                     zb[:, k * O + ti * W:k * O + ti * W + W],
                     yaccF[:, k * O + ti * W:k * O + ti * W + W],
                     inc=((b.VE, 1) if k == KD - 1 else None))

        b.wait("tensor", (b.WL, wl0), (b.LD, None))
        for ti in range(NTI):
            for k in range(KD):
                b.mm(gps[0:4, 0:W], gwb[:, k * 4:(k + 1) * 4],
                     yaccF[:, k * O + ti * W:k * O + (ti + 1) * W],
                     start=(k == 0), stop=(k == KD - 1),
                     wait=([(b.VE, None)] if k == 0 else ()),
                     inc=((b.MM, 1) if k == KD - 1 else None))
            b.op("vector", "tensor_copy", lgtb[0:4, ti * W:(ti + 1) * W],
                 gps[0:4, 0:W], wait=[(b.MM, b.MM.n)], inc=(b.VE, 1))
        E0, BS, EQ1, B2, TT_, CM = 0, 64, 128, 192, 256, 320
        MX, NMX, Z2, REC = 384, 400, 416, 432
        e0 = lambda s: gat[:, E0 + 4 * s:E0 + 4 * (s + 1)]
        for s in range(NSUB):
            b.op("tensor", "transpose", tps[:, 0:4],
                 lgtb[0:4, s * 128:(s + 1) * 128], identf[0:4, 0:4],
                 wait=[(b.VE, None), (b.LD, ld0), (b.SQ, None)], inc=(b.MM, 1))
            b.op("vector", "tensor_copy", e0(s), tps[:, 0:4],
                 wait=[(b.MM, b.MM.n)], inc=(b.VE, 1))
        vping(b)
        b.tt(ALU.add, gat[:, BS:BS + 64], gat[:, E0:E0 + 64], gbt[:],
             inc=(b.VE, 1))
        vping(b)
        for s in range(NSUB):
            b.op("vector", "reduce_max", gat[:, MX + s:MX + s + 1],
                 gat[:, BS + 4 * s:BS + 4 * (s + 1)], AX.X)
        for s in range(NSUB):
            b.op("vector", "tensor_scalar_mul", gat[:, NMX + s:NMX + s + 1],
                 gat[:, MX + s:MX + s + 1], -1.0,
                 inc=((b.VE, 1) if s == NSUB - 1 else None))
        for s in range(NSUB):
            b.ts(gat[:, EQ1 + 4 * s:EQ1 + 4 * (s + 1)],
                 gat[:, BS + 4 * s:BS + 4 * (s + 1)],
                 gat[:, MX + s:MX + s + 1], None, ALU.is_ge,
                 inc=((b.VE, 1) if s == NSUB - 1 else None))
        vping(b)
        b.stt(gat[:, B2:B2 + 64], gat[:, EQ1:EQ1 + 64], -1e30,
              gat[:, BS:BS + 64], ALU.mult, ALU.add, inc=(b.VE, 1))
        vping(b)
        for s in range(NSUB):
            b.op("vector", "reduce_max", gat[:, MX + s:MX + s + 1],
                 gat[:, B2 + 4 * s:B2 + 4 * (s + 1)], AX.X,
                 inc=((b.VE, 1) if s == NSUB - 1 else None))
        for s in range(NSUB):
            b.ts(gat[:, B2 + 4 * s:B2 + 4 * (s + 1)],
                 gat[:, B2 + 4 * s:B2 + 4 * (s + 1)],
                 gat[:, MX + s:MX + s + 1], None, ALU.is_ge,
                 inc=((b.VE, 1) if s == NSUB - 1 else None))
        vping(b)
        b.tt(ALU.add, gat[:, EQ1:EQ1 + 64], gat[:, EQ1:EQ1 + 64],
             gat[:, B2:B2 + 64], inc=(b.VE, 1))
        for s in range(NSUB):
            b.act(gat[:, BS + 4 * s:BS + 4 * (s + 1)],
                  gat[:, BS + 4 * s:BS + 4 * (s + 1)], AF.Exp,
                  bias=gat[:, NMX + s:NMX + s + 1],
                  wait=([(b.VE, None)] if s == 0 else ()),
                  inc=((b.SQ, 1) if s == NSUB - 1 else None))
        b.wait("vector", (b.SQ, b.SQ.n))
        b.tt(ALU.mult, gat[:, TT_:TT_ + 64], gat[:, BS:BS + 64],
             gat[:, EQ1:EQ1 + 64], inc=(b.VE, 1))
        vping(b)
        for s in range(NSUB):
            b.op("vector", "reduce_sum", gat[:, Z2 + s:Z2 + s + 1],
                 gat[:, TT_ + 4 * s:TT_ + 4 * (s + 1)], AX.X,
                 inc=((b.VE, 1) if s == NSUB - 1 else None))
        vping(b)
        b.op("vector", "reciprocal", gat[:, REC:REC + 16],
             gat[:, Z2:Z2 + 16], inc=(b.VE, 1))
        vping(b)
        for s in range(NSUB):
            b.ts(gat[:, CM + 4 * s:CM + 4 * (s + 1)],
                 gat[:, TT_ + 4 * s:TT_ + 4 * (s + 1)],
                 gat[:, REC + s:REC + s + 1], None, ALU.mult,
                 inc=((b.VE, 1) if s == NSUB - 1 else None))
        for s in range(NSUB):
            for e in range(4):
                b.op("tensor", "transpose", tpc[0:1, :],
                     gat[:, CM + 4 * s + e:CM + 4 * s + e + 1],
                     identf[:],
                     wait=[(b.VE, None), (b.SQ, None)], inc=(b.MM, 1))
                b.op("scalar", "copy", crow[0:1, e * O + s * 128:e * O + (s + 1) * 128],
                     tpc[0:1, :], wait=[(b.MM, b.MM.n)], inc=(b.SQ, 1))

        for e in range(4):
            b.dma("gpsimd", w1b[:].rearrange("p (k d) -> p k d", k=KD),
                  w1h[e].rearrange("(k p) d -> p k d", p=128),
                  wait=[(b.MM, None)])
            b.dma("gpsimd", w2b[:].rearrange("p (k d) -> p k d", k=KD),
                  w2h[e].rearrange("(k p) d -> p k d", p=128))
            wl_e = b.WL.n
            for ti in range(NTI):
                for o in range(KD):
                    pb = gps[:, (o % 2) * 512:(o % 2) * 512 + W]
                    for k in range(KD):
                        b.mm(pb, w1b[:, k * D + o * 128:k * D + (o + 1) * 128],
                             zb[:, k * O + ti * W:k * O + (ti + 1) * W],
                             start=(k == 0), stop=(k == KD - 1),
                             wait=([(b.WL, wl_e), (b.SQ, None), (b.VE, None)]
                                   if k == 0 else ()),
                             inc=((b.MM, 1) if k == KD - 1 else None))
                    b.op("scalar", "add", fmv(t1f, o, W), pb, ccol(e, 0, o),
                         wait=[(b.MM, b.MM.n)], inc=(b.SQ, 1))
                emit_ln(b, lambda k: fmv(t1f, k, W), lambda k: fmv(t1f, k, W),
                        W, t, mean=True, eps=1e-5,
                        gam=lambda k: ccol(e, 1, k), bet=lambda k: ccol(e, 2, k))
                for k in range(KD):
                    b.op("vector", "tensor_scalar_max", fmv(hb, k, W),
                         fmv(t1f, k, W), 0.0,
                         inc=((b.VE, 1) if k == KD - 1 else None))
                for o in range(KD):
                    pb = gps[:, (o % 2) * 512:(o % 2) * 512 + W]
                    for k in range(KD):
                        b.mm(pb, w2b[:, k * D + o * 128:k * D + (o + 1) * 128],
                             fmv(hb, k, W),
                             start=(k == 0), stop=(k == KD - 1),
                             wait=([(b.SQ, None), (b.VE, None)] if k == 0 else ()),
                             inc=((b.MM, 1) if k == KD - 1 else None))
                    b.op("scalar", "add", fmv(t1f, o, W), pb, ccol(e, 3, o),
                         wait=[(b.MM, b.MM.n)], inc=(b.SQ, 1))
                emit_ln(b, lambda k: fmv(t1f, k, W), lambda k: fmv(t1f, k, W),
                        W, t, mean=True, eps=1e-5,
                        gam=lambda k: ccol(e, 4, k), bet=lambda k: ccol(e, 5, k))
                b.mm(t["bc"][:, 0:W], onesb1r[:],
                     crow[0:1, e * O + ti * W:e * O + (ti + 1) * W],
                     start=True, stop=True, wait=[(b.VE, None), (b.SQ, None)],
                     inc=(b.MM, 1))
                mm_w = b.MM.n
                for k in range(KD):
                    b.tt(ALU.mult, fmv(t1f, k, W), fmv(t1f, k, W),
                         t["bc"][:, 0:W],
                         wait=([(b.MM, mm_w)] if k == 0 else ()))
                    b.tt(ALU.add, yaccF[:, k * O + ti * W:k * O + (ti + 1) * W],
                         yaccF[:, k * O + ti * W:k * O + (ti + 1) * W],
                         fmv(t1f, k, W),
                         inc=((b.VE, 1) if k == KD - 1 else None))

        for ti in range(NTI):
            b.dma("sync", xtt[:].rearrange("p (k t) -> p k t", k=KD),
                  rearr_in(xT[:, ti * W:(ti + 1) * W]),
                  wait=[(b.VE, None), (b.ST, b.ST.n)])
            ld = b.LD.n
            b.wait("vector", (b.LD, ld))
            for k in range(KD):
                b.tt(ALU.add, fmv(xtt, k, W), fmv(xtt, k, W),
                     yaccF[:, k * O + ti * W:k * O + (ti + 1) * W],
                     inc=((b.VE, 1) if k == KD - 1 else None))
            b.dma("gpsimd", xo[:, ti * W:(ti + 1) * W].rearrange("(k p) t -> p k t", p=128),
                  xtt[:].rearrange("p (k t) -> p k t", k=KD), sem=b.ST,
                  wait=[(b.VE, None)])
        b.wait("gpsimd", (b.ST, b.ST.n))


def stage_l1ffn(inp, xs):
    import ml_dtypes
    gw = np.ascontiguousarray(inp["L1_ffn_gate_W"])
    gbt4 = np.ascontiguousarray(np.tile(inp["L1_ffn_gate_b"][None, :], (128, 16)))
    identh = np.eye(128, dtype=ml_dtypes.bfloat16)
    w1h = np.ascontiguousarray(inp["L1_ffn_W1"])
    w2h = np.ascontiguousarray(inp["L1_ffn_W2"])
    cols = np.empty((128, 4 * 6 * KD), np.float32)
    for e in range(4):
        for pi, nm in enumerate(("b1", "g1", "be1", "b2", "g2", "be2")):
            cols[:, e * 48 + pi * 8:e * 48 + (pi + 1) * 8] = _vec8(inp[f"L1_ffn_{nm}"][e])
    maps = []
    for c in range(NCORE):
        bb, h = c // 2, c % 2
        maps.append({
            "xT": np.ascontiguousarray(xs[bb, h * O:(h + 1) * O].T),
            "gw": gw, "gbt4": gbt4, "identh": identh,
            "w1h": w1h, "w2h": w2h, "colh": cols,
        })
    return maps


def l1_ffn(inp, xs):
    maps = stage_l1ffn(inp, xs)
    r = run("l1ffn", build_l1ffn, maps)
    out = np.empty_like(xs)
    for c in range(NCORE):
        bb, h = c // 2, c % 2
        out[bb, h * O:(h + 1) * O] = r[c]["xo"].T
    return out


def build_gqa(rope, qknorm, qkvbias, lookahead):
    nc = bass.Bass("TRN2", target_bir_lowering=False, debug=False)
    S = T
    io = {}
    io["xkv"] = nc.dram_tensor("xkv", [D, S], F32, kind="ExternalInput").ap()
    io["x_tm"] = nc.dram_tensor("x_tm", [S, D], F32, kind="ExternalInput").ap()
    for nm, shp in (("wqh", [D, D]), ("wkh", [D, D]), ("wvh", [D, D]),
                    ("woh", [D, D]), ("bqc", [128, KD]), ("bkc", [128, KD]),
                    ("bvc", [128, D]), ("qwc", [128, 1]), ("kwc", [128, 1]),
                    ("hselh", [128, 2]), ("hselth", [2, 128]),
                    ("ctabq", [128, S]), ("stabq", [128, S]),
                    ("ctabk", [128, S]), ("stabk", [128, S]),
                    ("dmaskh", [4, 128, 512])):
        io[nm] = nc.dram_tensor(nm, shp, F32, kind="ExternalInput").ap()
    io["identh"] = nc.dram_tensor("identh", [128, 128], BF16, kind="ExternalInput").ap()
    io["cscr"] = nc.dram_tensor("cscr", [8, 128, S], BF16, kind="Internal").ap()
    io["xo"] = nc.dram_tensor("xo", [S, D], F32, kind="ExternalOutput").ap()
    b = Bld(nc)
    emit_gqa(nc, b, io, rope, qknorm, qkvbias, lookahead)
    with ExitStack() as stack:
        b.finalize(stack)
    return nc


def emit_gqa(nc, b, io, rope, qknorm, qkvbias, lookahead):
    S = T
    (xkv, x_tm, wqh, wkh, wvh, woh, bqc, bkc, bvc, qwc, kwc, hselh, hselth,
     ctabq, stabq, ctabk, stabk, dmaskh, identh, cscr, xo) = (io[k] for k in (
        "xkv", "x_tm", "wqh", "wkh", "wvh", "woh", "bqc", "bkc", "bvc", "qwc",
        "kwc", "hselh", "hselth", "ctabq", "stabq", "ctabk", "stabk", "dmaskh",
        "identh", "cscr", "xo"))
    WL_ = 256
    QG = 2
    NQT = O // 128
    with ExitStack() as stack:
        en = stack.enter_context
        t = {}
        t["ones_col"] = en(SB(nc, "ones_col", [128, 1], F32))
        onesb = en(SB(nc, "onesb", [128, 1], BF16))
        t["ones1r"] = en(SB(nc, "ones1r", [1, 128], F32))
        t["rows"] = en(PS(nc, "rows", [2, 512], F32))
        t["rowsb"] = en(SB(nc, "rowsb", [2, 1024], F32))
        t["bc"] = en(PS(nc, "bc", [128, 1024], F32))
        pbuf = en(SB(nc, "pbuf", [128, QG * S], BF16))
        ptb = en(SB(nc, "ptb", [128, 32 * 128 * QG], BF16))
        pb32 = pbuf[:].bitcast(F32)
        xc = lambda: pb32[:, 0:KD * WL_]
        t["sq"] = ptb[:].bitcast(F32)[:, 0:KD * WL_]
        sq2v = ptb[:].bitcast(F32)[:, KD * WL_:2 * KD * WL_]
        zb = en(SB(nc, "zb", [128, KD * S], BF16))
        kof = en(SB(nc, "kof", [128, S], F32))
        qof = kof[:, 0:O]
        kb = en(SB(nc, "kb", [128, S], BF16))
        qb = en(SB(nc, "qb", [128, O], BF16))
        vo = en(SB(nc, "vo", [128, 32 * 128], BF16))
        xtm = vo[:].bitcast(F32)[:, 0:D]
        outb = vo[:].bitcast(F32)[:, D:2 * D]
        wb = en(SB(nc, "wb", [128, 3 * KD * 128], BF16))
        wob = en(SB(nc, "wob", [128, KD * D], BF16))
        bvb = (en(SB(nc, "bvb", [128, D], F32)) if qkvbias else None)
        colv = en(SB(nc, "colv", [128, 2 * KD + 4], F32))
        hsel = en(SB(nc, "hsel", [128, 2], F32))
        hselt = en(SB(nc, "hselt", [2, 128], F32))
        if rope:
            ctq = en(SB(nc, "ctq", [128, O], BF16))
            stq = en(SB(nc, "stq", [128, O], BF16))
            ctk = en(SB(nc, "ctk", [128, S], BF16))
            stk = en(SB(nc, "stk", [128, S], BF16))
        else:
            ctq = stq = ctk = stk = None
        dmask = en(SB(nc, "dmask", [128, 4 * 512], BF16))
        ssb = en(SB(nc, "ssb", [128, 512], F32))
        bnd = en(SB(nc, "bnd", [2, O], BF16))
        k2 = en(SB(nc, "k2", [2, 16], F32))
        bcol = en(SB(nc, "bcol", [128, 32], F32))
        zrec = en(SB(nc, "zrec", [1, QG * 128], F32))
        ident = en(SB(nc, "ident", [128, 128], BF16))
        ctxh = en(SB(nc, "ctxh", [128, O], BF16))
        ctxs = en(SB(nc, "ctxs", [128, KD * 128], BF16))
        pss = en(PS(nc, "pss", [128, 3 * 512], F32))
        ptp = en(PS(nc, "ptp", [128, 512], BF16))
        cpv = en(PS(nc, "cpv", [128, 512], F32))

        b.op("vector", "memset", t["ones_col"][:], 1.0)
        b.op("vector", "memset", onesb[:], 1.0)
        b.op("vector", "memset", t["ones1r"][:], 1.0, inc=(b.VE, 1))
        b.dma("sync", colv[:, 0:KD], bqc)
        b.dma("sync", colv[:, KD:2 * KD], bkc)
        b.dma("sync", colv[:, 2 * KD:2 * KD + 1], qwc)
        b.dma("sync", colv[:, 2 * KD + 1:2 * KD + 2], kwc)
        b.dma("sync", ident[:], identh)
        if qkvbias:
            b.dma("sync", bvb[:], bvc)
        b.dma("sync", hsel[:], hselh)
        b.dma("sync", hselt[:], hselth)
        b.dma("gpsimd", dmask[:].rearrange("p (f c) -> p f c", f=4),
              dmaskh.rearrange("f p c -> p f c"))
        ld0 = b.LD.n
        if rope:
            b.dma("gpsimd", ctk[:], ctabk)
            b.dma("gpsimd", stk[:], stabk)
        b.dma("gpsimd", wob[:].rearrange("p (k d) -> p k d", k=KD),
              woh.rearrange("(k p) d -> p k d", p=128))
        wl0 = b.WL.n
        b.wait("vector", (b.LD, ld0))

        for ti in range(S // WL_):
            b.dma("sync", xc().rearrange("p (k t) -> p k t", k=KD),
                  rearr_in(xkv[:, ti * WL_:(ti + 1) * WL_]), wait=[(b.VE, None)])
            ld = b.LD.n
            b.wait("scalar", (b.LD, ld))
            b.wait("tensor", (b.LD, ld))
            emit_ln(b, lambda k: fmv(xc(), k, WL_),
                    lambda k: zb[:, k * S + ti * WL_:k * S + ti * WL_ + WL_],
                    WL_, t, mean=True, eps=1e-5, sq2=sq2v)

        for o in range(8):
            b.dma("gpsimd", wb[:, 0:KD * 128].rearrange("p (k c) -> p k c", k=KD),
                  wqh[:, o * 128:(o + 1) * 128].rearrange("(k p) c -> p k c", p=128),
                  wait=[(b.MM, None)])
            b.dma("gpsimd", wb[:, KD * 128:2 * KD * 128].rearrange("p (k c) -> p k c", k=KD),
                  wkh[:, o * 128:(o + 1) * 128].rearrange("(k p) c -> p k c", p=128))
            b.dma("gpsimd", wb[:, 2 * KD * 128:3 * KD * 128].rearrange("p (k c) -> p k c", k=KD),
                  wvh[:, o * 128:(o + 1) * 128].rearrange("(k p) c -> p k c", p=128))
            wl_o = b.WL.n
            for kt in range(8):
                pb = pss[:, (kt % 2) * 512:(kt % 2) * 512 + 512]
                for k in range(KD):
                    b.mm(pb, wb[:, KD * 128 + k * 128:KD * 128 + (k + 1) * 128],
                         zb[:, k * S + kt * 512:k * S + (kt + 1) * 512],
                         start=(k == 0), stop=(k == KD - 1),
                         wait=([(b.WL, wl_o), (b.SQ, None), (b.VE, None)]
                               if k == 0 else ()),
                         inc=((b.MM, 1) if k == KD - 1 else None))
                if qkvbias:
                    b.op("scalar", "add", kof[:, kt * 512:(kt + 1) * 512], pb,
                         colv[:, KD + o:KD + o + 1], wait=[(b.MM, b.MM.n)],
                         inc=(b.SQ, 1))
                else:
                    b.op("scalar", "copy", kof[:, kt * 512:(kt + 1) * 512], pb,
                         wait=[(b.MM, b.MM.n)], inc=(b.SQ, 1))
            sq_kproj = b.SQ.n
            if qknorm:
                for j in range(S // 512):
                    b.act(ssb[:, 0:512], kof[:, j * 512:(j + 1) * 512],
                          AF.Square,
                          wait=([(b.SQ, sq_kproj), (b.MM, None), (b.VE, None)]
                                if j == 0 else [(b.MM, None)]),
                          inc=(b.SQ, 1))
                    b.mm(t["rows"][0:2, 0:512], hsel[:], ssb[:, 0:512],
                         start=True, stop=True, wait=[(b.SQ, b.SQ.n)],
                         inc=(b.MM, 1))
                    b.ts(t["rowsb"][0:2, 0:512], t["rows"][0:2, 0:512],
                         1.0 / 64, 1e-6, ALU.mult, ALU.add,
                         wait=[(b.MM, b.MM.n)], inc=(b.VE, 1))
                    b.act(t["rowsb"][0:2, 0:512], t["rowsb"][0:2, 0:512],
                          AF.Ln, wait=[(b.VE, b.VE.n)])
                    b.act(t["rowsb"][0:2, 0:512], t["rowsb"][0:2, 0:512],
                          AF.Exp, scale=-0.5, inc=(b.SQ, 1))
                    b.mm(t["bc"][:, 0:512], hselt[:], t["rowsb"][0:2, 0:512],
                         start=True, stop=True,
                         wait=[(b.SQ, b.SQ.n), (b.VE, None)], inc=(b.MM, 1))
                    b.stt(kof[:, j * 512:(j + 1) * 512],
                          kof[:, j * 512:(j + 1) * 512],
                          colv[:, 2 * KD + 1:2 * KD + 2], t["bc"][:, 0:512],
                          ALU.mult, ALU.mult, wait=[(b.MM, b.MM.n)],
                          inc=(b.VE, 1))

            def rope_pass(srcf, dstb, ctab, stab, L):
                ta = lambda r0: pb32[r0:r0 + 32, 0:2048]
                tb = lambda r0: pb32[r0:r0 + 32, 2048:4096]
                for j in range(L // 2048):
                    J = slice(j * 2048, (j + 1) * 2048)
                    sl = lambda r0: srcf[r0:r0 + 32, J]
                    ct = lambda r0: ctab[r0:r0 + 32, J]
                    st = lambda r0: stab[r0:r0 + 32, J]
                    w0 = [(b.SQ, None), (b.MM, None), (b.VE, None)] if j == 0 else ()
                    if rope:
                        b.tt(ALU.mult, ta(0), sl(0), ct(0), wait=w0)
                        b.tt(ALU.mult, tb(0), sl(32), st(32))
                        b.tt(ALU.mult, ta(64), sl(64), ct(64))
                        b.tt(ALU.mult, tb(64), sl(96), st(96))
                        b.tt(ALU.subtract, dstb[0:32, J], ta(0), tb(0))
                        b.tt(ALU.subtract, dstb[64:96, J], ta(64), tb(64))
                        b.tt(ALU.mult, ta(0), sl(0), st(0))
                        b.tt(ALU.mult, tb(0), sl(32), ct(32))
                        b.tt(ALU.mult, ta(64), sl(64), st(64))
                        b.tt(ALU.mult, tb(64), sl(96), ct(96))
                        b.tt(ALU.add, dstb[32:64, J], tb(0), ta(0))
                        b.tt(ALU.add, dstb[96:128, J], tb(64), ta(64),
                             inc=(b.VE, 1))
                    else:
                        b.op("vector", "tensor_copy", dstb[:, J], srcf[:, J],
                             wait=w0, inc=(b.VE, 1))
            rope_pass(kof, kb, ctk, stk, S)
            for s in range(32):
                pb = pss[:, (s % 2) * 512:(s % 2) * 512 + 128]
                b.mm(pb, zb[:, 0 * S + s * 128:0 * S + (s + 1) * 128],
                     wb[:, 2 * KD * 128:2 * KD * 128 + 128], start=True,
                     stop=False, wait=[(b.VE, None), (b.SQ, None)])
                for k in range(1, KD):
                    b.mm(pb, zb[:, k * S + s * 128:k * S + (s + 1) * 128],
                         wb[:, 2 * KD * 128 + k * 128:2 * KD * 128 + (k + 1) * 128],
                         start=False, stop=(k == KD - 1),
                         inc=((b.MM, 1) if k == KD - 1 else None))
                if qkvbias:
                    b.tt(ALU.add, vo[:, s * 128:(s + 1) * 128], pb,
                         bvb[:, o * 128:(o + 1) * 128], wait=[(b.MM, b.MM.n)],
                         inc=(b.VE, 1))
                else:
                    b.op("vector", "tensor_copy", vo[:, s * 128:(s + 1) * 128],
                         pb, wait=[(b.MM, b.MM.n)], inc=(b.VE, 1))
            for j in range(S // 512):
                b.act(ssb[:, 0:512], kb[:, j * 512:(j + 1) * 512], AF.Square,
                      wait=([(b.VE, None), (b.MM, None)] if j == 0
                            else [(b.MM, None)]), inc=(b.SQ, 1))
                b.mm(t["rows"][0:2, 0:512], hsel[:], ssb[:, 0:512],
                     start=True, stop=True,
                     wait=[(b.SQ, b.SQ.n), (b.VE, None)], inc=(b.MM, 1))
                b.op("vector", "reduce_max", k2[0:2, j:j + 1],
                     t["rows"][0:2, 0:512], AX.X, wait=[(b.MM, b.MM.n)],
                     inc=(b.VE, 1))
            b.op("vector", "reduce_max", k2[0:2, 8:9], k2[0:2, 0:8], AX.X,
                 inc=(b.VE, 1))
            b.act(k2[0:2, 9:10], k2[0:2, 8:9], AF.Sqrt, wait=[(b.VE, b.VE.n)],
                  inc=(b.SQ, 1))
            for q_base in (0, O):
                qb2 = q_base // 512
                if rope:
                    b.dma("gpsimd", ctq[:], ctabq[:, q_base:q_base + O],
                          wait=[(b.VE, None), (b.SQ, None)])
                    b.dma("gpsimd", stq[:], stabq[:, q_base:q_base + O])
                wl_q = b.WL.n
                for qt4 in range(4):
                    pb = pss[:, (qt4 % 2) * 512:(qt4 % 2) * 512 + 512]
                    for k in range(KD):
                        b.mm(pb, wb[:, k * 128:(k + 1) * 128],
                             zb[:, k * S + q_base + qt4 * 512:k * S + q_base + (qt4 + 1) * 512],
                             start=(k == 0), stop=(k == KD - 1),
                             wait=([(b.SQ, None), (b.VE, None)] if k == 0 else ()),
                             inc=((b.MM, 1) if k == KD - 1 else None))
                    if qkvbias:
                        b.op("scalar", "add", qof[:, qt4 * 512:(qt4 + 1) * 512],
                             pb, colv[:, o:o + 1], wait=[(b.MM, b.MM.n)],
                             inc=(b.SQ, 1))
                    else:
                        b.op("scalar", "copy", qof[:, qt4 * 512:(qt4 + 1) * 512],
                             pb, wait=[(b.MM, b.MM.n)], inc=(b.SQ, 1))
                sq_qproj = b.SQ.n
                if qknorm:
                    for j in range(O // 512):
                        b.act(ssb[:, 0:512], qof[:, j * 512:(j + 1) * 512],
                              AF.Square,
                              wait=([(b.SQ, sq_qproj), (b.MM, None), (b.VE, None)]
                                    if j == 0 else [(b.MM, None)]),
                              inc=(b.SQ, 1))
                        b.mm(t["rows"][0:2, 0:512], hsel[:], ssb[:, 0:512],
                             start=True, stop=True, wait=[(b.SQ, b.SQ.n)],
                             inc=(b.MM, 1))
                        b.ts(t["rowsb"][0:2, 0:512], t["rows"][0:2, 0:512],
                             1.0 / 64, 1e-6, ALU.mult, ALU.add,
                             wait=[(b.MM, b.MM.n)], inc=(b.VE, 1))
                        b.act(t["rowsb"][0:2, 0:512], t["rowsb"][0:2, 0:512],
                              AF.Ln, wait=[(b.VE, b.VE.n)])
                        b.act(t["rowsb"][0:2, 0:512], t["rowsb"][0:2, 0:512],
                              AF.Exp, scale=-0.5, inc=(b.SQ, 1))
                        b.mm(t["bc"][:, 0:512], hselt[:], t["rowsb"][0:2, 0:512],
                             start=True, stop=True,
                             wait=[(b.SQ, b.SQ.n), (b.VE, None)], inc=(b.MM, 1))
                        b.stt(qof[:, j * 512:(j + 1) * 512],
                              qof[:, j * 512:(j + 1) * 512],
                              colv[:, 2 * KD:2 * KD + 1], t["bc"][:, 0:512],
                              ALU.mult, ALU.mult, wait=[(b.MM, b.MM.n)],
                              inc=(b.VE, 1))
                rope_pass(qof, qb, ctq, stq, O)
                for j in range(O // 512):
                    b.act(ssb[:, 0:512], qb[:, j * 512:(j + 1) * 512],
                          AF.Square,
                          wait=([(b.VE, None), (b.MM, None)] if j == 0
                                else [(b.MM, None)]), inc=(b.SQ, 1))
                    b.mm(t["rows"][0:2, 0:512], hsel[:], ssb[:, 0:512],
                         start=True, stop=True,
                         wait=[(b.SQ, b.SQ.n), (b.VE, None)], inc=(b.MM, 1))
                    b.act(bnd[0:2, j * 512:(j + 1) * 512],
                          t["rows"][0:2, 0:512], AF.Sqrt,
                          wait=[(b.MM, b.MM.n)], inc=(b.SQ, 1))
                b.ts(bnd[0:2, 0:O], bnd[0:2, 0:O], k2[0:2, 9:10],
                     -1.03 / 8.0, ALU.mult, ALU.mult, wait=[(b.SQ, b.SQ.n)],
                     inc=(b.VE, 1))
                ve_bnd = b.VE.n
                for qt in range(16):
                    b.op("tensor", "transpose", ptp[:, 4 * qt:4 * qt + 2],
                         bnd[0:2, qt * 128:(qt + 1) * 128], ident[0:2, 0:2],
                         wait=([(b.VE, ve_bnd), (b.SQ, None)] if qt == 0 else ()),
                         inc=(b.MM, 1))
                mm_bt = b.MM.n
                b.op("scalar", "copy", bcol[:, 0:32:2], ptp[:, 0:64:4],
                     wait=[(b.MM, mm_bt)])
                b.op("scalar", "copy", bcol[:, 1:32:2], ptp[:, 1:64:4],
                     inc=(b.SQ, 1))
                sq_bias = b.SQ.n
                for hh in range(2):
                    hb_ = 64 * hh
                    for qg in range(NQT // QG):
                        for qi in range(QG):
                            qt = qg * QG + qi
                            ktd, ph_ = qb2 + qt // 4, qt % 4
                            mt = ktd + (1 if (lookahead and ph_ == 3) else 0)
                            kts = [kt for kt in range(8) if kt <= min(mt, 7)]
                            for kt in kts:
                                par = (qt * 8 + kt) % 2
                                pA = pss[:, par * 512:par * 512 + 512]
                                b.mm(pA, qb[hb_:hb_ + 64, qt * 128:(qt + 1) * 128],
                                     kb[hb_:hb_ + 64, kt * 512:(kt + 1) * 512],
                                     start=True, stop=True,
                                     wait=[(b.SQ, None), (b.VE, None)],
                                     inc=(b.MM, 1))
                                mm_s = b.MM.n
                                if lookahead:
                                    masked = (kt == mt and not (ph_ == 3 and mt == ktd))
                                    mphase = (ph_ + 1) % 4
                                else:
                                    masked = (kt == ktd)
                                    mphase = ph_
                                bias = bcol[:, qt * 2 + hh:qt * 2 + hh + 1]
                                if masked:
                                    b.tt(ALU.add, ssb[:, 0:512], pA,
                                         dmask[:, mphase * 512:(mphase + 1) * 512],
                                         wait=[(b.MM, mm_s), (b.SQ, None)],
                                         inc=(b.VE, 1))
                                    srcx = ssb[:, 0:512]
                                    wx = [(b.VE, b.VE.n), (b.SQ, sq_bias)]
                                else:
                                    srcx = pA
                                    wx = [(b.MM, mm_s), (b.SQ, sq_bias)]
                                b.act(pbuf[:, qi * S + kt * 512:qi * S + (kt + 1) * 512],
                                      srcx, AF.Exp, scale=1.0 / 8.0, bias=bias,
                                      wait=wx, inc=(b.SQ, 1))
                            for kt in range(8):
                                if kt not in kts:
                                    b.op("vector", "memset",
                                         pbuf[:, qi * S + kt * 512:qi * S + (kt + 1) * 512],
                                         0.0, inc=(b.VE, 1))
                        for ks in range(32):
                            for qi in range(QG):
                                b.op("tensor", "transpose",
                                     ptp[:, qi * 128:(qi + 1) * 128],
                                     pbuf[:, qi * S + ks * 128:qi * S + (ks + 1) * 128],
                                     ident[:],
                                     wait=([(b.SQ, None), (b.VE, None)] if qi == 0 else ()),
                                     inc=((b.MM, 1) if qi == QG - 1 else None))
                            mm_tr = b.MM.n
                            if ks % 2 == 0:
                                b.op("vector", "tensor_copy",
                                     ptb[:, ks * QG * 128:(ks + 1) * QG * 128],
                                     ptp[:, 0:QG * 128], wait=[(b.MM, mm_tr)],
                                     inc=(b.VE, 1))
                            else:
                                b.op("scalar", "copy",
                                     ptb[:, ks * QG * 128:(ks + 1) * QG * 128],
                                     ptp[:, 0:QG * 128], wait=[(b.MM, mm_tr)],
                                     inc=(b.SQ, 1))
                        ve_tb, sq_tb = b.VE.n, b.SQ.n
                        for ks in range(32):
                            b.mm(t["rows"][0:1, 0:QG * 128], onesb[:],
                                 ptb[:, ks * QG * 128:(ks + 1) * QG * 128],
                                 start=(ks == 0), stop=(ks == 31),
                                 wait=([(b.VE, ve_tb), (b.SQ, sq_tb)] if ks == 0 else ()),
                                 inc=((b.MM, 1) if ks == 31 else None))
                        b.op("vector", "reciprocal", zrec[0:1, :],
                             t["rows"][0:1, 0:QG * 128], wait=[(b.MM, b.MM.n)],
                             inc=(b.VE, 1))
                        b.mm(t["bc"][:, 0:QG * 128], t["ones1r"][:], zrec[0:1, :],
                             start=True, stop=True, wait=[(b.VE, b.VE.n)],
                             inc=(b.MM, 1))
                        mm_zbc = b.MM.n
                        for ks in range(32):
                            b.mm(cpv[0:64, 0:QG * 128],
                                 vo[:, ks * 128 + hb_:ks * 128 + hb_ + 64],
                                 ptb[:, ks * QG * 128:(ks + 1) * QG * 128],
                                 start=(ks == 0), stop=(ks == 31),
                                 wait=([(b.SQ, None)] if ks == 0 else ()),
                                 inc=((b.MM, 1) if ks == 31 else None))
                        b.op("scalar", "copy", ssb[0:64, 0:QG * 128],
                             cpv[0:64, 0:QG * 128], wait=[(b.MM, b.MM.n)],
                             inc=(b.SQ, 1))
                        b.tt(ALU.mult,
                             ctxh[hb_:hb_ + 64, qg * QG * 128:(qg + 1) * QG * 128],
                             ssb[0:64, 0:QG * 128], t["bc"][0:64, 0:QG * 128],
                             wait=[(b.SQ, b.SQ.n), (b.MM, mm_zbc)], inc=(b.VE, 1))
                b.dma("gpsimd", cscr[o][:, q_base:q_base + O], ctxh[:],
                      sem=b.ST, wait=[(b.VE, None)])
                b.wait("vector", (b.ST, b.ST.n))

        st_ctx = b.ST.n
        for s in range(S // 128):
            b.dma("sync", ctxs[:].rearrange("p (k c) -> p k c", k=KD),
                  cscr[:, :, s * 128:(s + 1) * 128].rearrange("k p c -> p k c"),
                  wait=[(b.ST, st_ctx), (b.MM, None)])
            b.dma("sync", xtm, x_tm[s * 128:(s + 1) * 128, :],
                  wait=[(b.VE, None)])
            ld_s = b.LD.n
            for n in range(2):
                pb = pss[:, n * 512:n * 512 + 512]
                for hh in range(KD):
                    b.mm(pb, ctxs[:, hh * 128:(hh + 1) * 128],
                         wob[:, hh * D + n * 512:hh * D + (n + 1) * 512],
                         start=(hh == 0), stop=(hh == KD - 1),
                         wait=([(b.LD, ld_s), (b.VE, None)] if hh == 0 else ()),
                         inc=((b.MM, 1) if hh == KD - 1 else None))
                b.tt(ALU.add, outb[:, n * 512:(n + 1) * 512],
                     pb, xtm[:, n * 512:(n + 1) * 512],
                     wait=[(b.MM, b.MM.n), (b.ST, b.ST.n)], inc=(b.VE, 1))
            b.dma("gpsimd", xo[s * 128:(s + 1) * 128, :], outb,
                  sem=b.ST, wait=[(b.VE, None)])
        b.wait("gpsimd", (b.ST, b.ST.n))


def stage_gqa(inp, xs, layer):
    import ml_dtypes
    S = T
    identh = np.eye(128, dtype=ml_dtypes.bfloat16)
    zcol = np.zeros((128, 1), np.float32)
    zcol8 = np.zeros((128, KD), np.float32)
    if layer == 2:
        wqh = np.ascontiguousarray(inp["L2_attn_Wq"])
        wkh = np.ascontiguousarray(inp["L2_attn_Wk"])
        wvh = np.ascontiguousarray(inp["L2_attn_Wv"])
        woh = np.ascontiguousarray(inp["L2_attn_Wo"])
        bqc, bkc, bvc = zcol8, zcol8, np.zeros((128, D), np.float32)
        qwc = np.concatenate([inp["L2_attn_q_norm_w"]] * 2).reshape(128, 1)
        kwc = np.concatenate([inp["L2_attn_k_norm_w"]] * 2).reshape(128, 1)
    else:
        wqh = np.ascontiguousarray(inp["L3_attn_Wq"].T)
        wkh = np.ascontiguousarray(inp["L3_attn_Wk"].T)
        wvh = np.ascontiguousarray(inp["L3_attn_Wv"].T)
        woh = np.eye(D, dtype=np.float32)
        bqc = _vec8(inp["L3_attn_bq"])
        bkc = _vec8(inp["L3_attn_bk"])
        bvc = np.ascontiguousarray(np.tile(inp["L3_attn_bv"][None, :], (128, 1)))
        qwc, kwc = zcol, zcol
    hselh = np.zeros((128, 2), np.float32)
    hselh[0:64, 0] = 1.0
    hselh[64:128, 1] = 1.0
    hselth = np.ascontiguousarray(hselh.T)
    inv = 1.0 / 10000.0 ** (np.arange(0, 64, 2, dtype=np.float64) / 64)
    ii = np.arange(128)[:, None]
    jj = np.arange(512)[None, :]
    dmaskh = np.stack([np.where(jj <= ii + 128 * p, 0.0, -1e30)
                       for p in range(4)]).astype(np.float32)
    kg = np.arange(S).astype(np.float64)
    freqs = np.tile(inv, 4)[:, None]
    ctab = np.cos(kg[None, :] * freqs).astype(np.float32)
    stab = np.sin(kg[None, :] * freqs).astype(np.float32)
    shared = {
        "wqh": wqh, "wkh": wkh, "wvh": wvh, "woh": woh,
        "bqc": bqc, "bkc": bkc, "bvc": bvc, "qwc": qwc, "kwc": kwc,
        "hselh": hselh, "hselth": hselth,
        "ctabq": ctab, "stabq": stab, "ctabk": ctab, "stabk": stab,
        "dmaskh": dmaskh, "identh": identh,
    }
    maps = []
    for c in range(NCORE):
        bb = c // 2
        m = dict(shared)
        m["xkv"] = np.ascontiguousarray(xs[bb].T)
        m["x_tm"] = np.ascontiguousarray(xs[bb])
        maps.append(m)
    return maps


def l2_attn(inp, xs):
    maps = stage_gqa(inp, xs, 2)
    r = run("l2attn", lambda: build_gqa(True, True, False, 0), maps)
    out = np.empty_like(xs)
    for bb in range(B):
        out[bb] = r[2 * bb]["xo"]
    return out


def l3_attn(inp, xs):
    maps = stage_gqa(inp, xs, 3)
    r = run("l3attn", lambda: build_gqa(False, False, True, 128), maps)
    out = np.empty_like(xs)
    for bb in range(B):
        out[bb] = r[2 * bb]["xo"]
    return out


def build_l2ffn():
    nc = bass.Bass("TRN2", target_bir_lowering=False, debug=False)
    OE = O + 1
    xTe = nc.dram_tensor("xTe", [D, OE], F32, kind="ExternalInput").ap()
    wrt = nc.dram_tensor("wrt", [D, D], F32, kind="ExternalInput").ap()
    wkt = nc.dram_tensor("wkt", [D, D], F32, kind="ExternalInput").ap()
    wvt = nc.dram_tensor("wvt", [D, D], F32, kind="ExternalInput").ap()
    murh = nc.dram_tensor("murh", [128, KD], F32, kind="ExternalInput").ap()
    mukh = nc.dram_tensor("mukh", [128, KD], F32, kind="ExternalInput").ap()
    xo = nc.dram_tensor("xo", [D, O], F32, kind="ExternalOutput").ap()
    b = Bld(nc)
    emit_l2ffn(nc, b, xTe, wrt, wkt, wvt, murh, mukh, xo)
    with ExitStack() as stack:
        b.finalize(stack)
    return nc


def emit_l2ffn(nc, b, xTe, wrt, wkt, wvt, murh, mukh, xo):
    OE = O + 1
    W = 512
    NTI = O // W
    with ExitStack() as stack:
        en = stack.enter_context
        t = {}
        t["ones_col"] = en(SB(nc, "ones_col", [128, 1], F32))
        t["ones1r"] = en(SB(nc, "ones1r", [1, 128], F32))
        t["rows"] = en(PS(nc, "rows", [1, 1024], F32))
        t["rowsb"] = en(SB(nc, "rowsb", [1, 1024], F32))
        t["bc"] = en(PS(nc, "bc", [128, 1024], F32))
        t["sq"] = en(SB(nc, "sq", [128, KD * W], F32))
        t["sq2"] = en(SB(nc, "sq2", [128, KD * W], F32))
        xc = en(SB(nc, "xc", [128, KD * W], F32))
        zbuf = en(SB(nc, "zbuf", [128, KD * OE], BF16))
        wr = en(SB(nc, "wr", [128, KD * D], BF16))
        wk = en(SB(nc, "wk", [128, KD * D], BF16))
        wv = en(SB(nc, "wv", [128, KD * D], BF16))
        mucol = en(SB(nc, "mucol", [128, 4 * KD], F32))
        xrb = en(SB(nc, "xrb", [128, KD * W], BF16))
        xkb = en(SB(nc, "xkb", [128, KD * W], BF16))
        rb = en(SB(nc, "rb", [128, KD * W], F32))
        ka = en(SB(nc, "ka", [128, KD * W], BF16))
        ot = en(SB(nc, "ot", [128, KD * W], F32))
        gps = en(PS(nc, "gps", [128, 1024], F32))

        b.op("vector", "memset", t["ones_col"][:], 1.0)
        b.op("vector", "memset", t["ones1r"][:], 1.0, inc=(b.VE, 1))
        b.dma("sync", mucol[:, 0:KD], murh)
        b.dma("sync", mucol[:, KD:2 * KD], mukh)
        ld0 = b.LD.n
        b.dma("gpsimd", wr[:].rearrange("p (k d) -> p k d", k=KD),
              wrt.rearrange("(k p) d -> p k d", p=128))
        b.dma("gpsimd", wk[:].rearrange("p (k d) -> p k d", k=KD),
              wkt.rearrange("(k p) d -> p k d", p=128))
        b.dma("gpsimd", wv[:].rearrange("p (k d) -> p k d", k=KD),
              wvt.rearrange("(k p) d -> p k d", p=128))
        wl0 = b.WL.n
        b.ts(mucol[:, 2 * KD:3 * KD], mucol[:, 0:KD], -1.0, 1.0, ALU.mult,
             ALU.add, wait=[(b.LD, ld0)])
        b.ts(mucol[:, 3 * KD:4 * KD], mucol[:, KD:2 * KD], -1.0, 1.0, ALU.mult,
             ALU.add, inc=(b.VE, 1))

        tiles = [(0, 512), (512, 512), (1024, 512), (1536, 512), (2048, 1)]
        for (c0, WW) in tiles:
            b.dma("sync", xc[:].rearrange("p (k t) -> p k t", k=KD)[:, :, 0:WW],
                  rearr_in(xTe[:, c0:c0 + WW]), wait=[(b.VE, None)],
                  **({"allow_slow_non_contiguous": True} if WW == 1 else {}))
            ld = b.LD.n
            b.wait("scalar", (b.LD, ld))
            b.wait("tensor", (b.LD, ld))
            emit_ln(b, lambda k: xc[:, k * W:k * W + WW],
                    lambda k: zbuf[:, k * OE + c0:k * OE + c0 + WW],
                    WW, t, mean=True, eps=1e-5)

        b.wait("tensor", (b.WL, wl0))
        for ti in range(NTI):
            c0 = ti * W
            zs = lambda k: zbuf[:, k * OE + 1 + c0:k * OE + 1 + c0 + W]
            zp = lambda k: zbuf[:, k * OE + c0:k * OE + c0 + W]
            for k in range(KD):
                b.op("vector", "tensor_scalar_mul", fmv(t["sq"], k, W), zs(k),
                     mucol[:, k:k + 1],
                     wait=([(b.MM, None), (b.SQ, None)] if k == 0 else ()))
                b.stt(fmv(xrb, k, W), zp(k), mucol[:, 2 * KD + k:2 * KD + k + 1],
                      fmv(t["sq"], k, W), ALU.mult, ALU.add)
                b.op("vector", "tensor_scalar_mul", fmv(t["sq"], k, W), zs(k),
                     mucol[:, KD + k:KD + k + 1])
                b.stt(fmv(xkb, k, W), zp(k), mucol[:, 3 * KD + k:3 * KD + k + 1],
                      fmv(t["sq"], k, W), ALU.mult, ALU.add,
                      inc=((b.VE, 1) if k == KD - 1 else None))
            for o in range(KD):
                pb = gps[:, (o % 2) * 512:(o % 2) * 512 + 512]
                for k in range(KD):
                    b.mm(pb, wr[:, k * D + o * 128:k * D + (o + 1) * 128],
                         fmv(xrb, k, W), start=(k == 0), stop=(k == KD - 1),
                         wait=([(b.VE, None), (b.SQ, None)] if k == 0 else ()),
                         inc=((b.MM, 1) if k == KD - 1 else None))
                b.act(fmv(rb, o, W), pb, AF.Sigmoid, wait=[(b.MM, b.MM.n)],
                      inc=(b.SQ, 1))
            for o in range(KD):
                pb = gps[:, (o % 2) * 512:(o % 2) * 512 + 512]
                for k in range(KD):
                    b.mm(pb, wk[:, k * D + o * 128:k * D + (o + 1) * 128],
                         fmv(xkb, k, W), start=(k == 0), stop=(k == KD - 1),
                         wait=([(b.SQ, None), (b.VE, None)] if k == 0 else ()),
                         inc=((b.MM, 1) if k == KD - 1 else None))
                mm_k = b.MM.n
                b.op("vector", "tensor_scalar_max", fmv(t["sq"], o, W), pb, 0.0,
                     wait=[(b.MM, mm_k)], inc=(b.VE, 1))
                b.act(fmv(ka, o, W), fmv(t["sq"], o, W), AF.Square,
                      wait=[(b.VE, b.VE.n)], inc=(b.SQ, 1))
            b.dma("sync", xc[:].rearrange("p (k t) -> p k t", k=KD),
                  rearr_in(xTe[:, 1 + c0:1 + c0 + W]),
                  wait=[(b.ST, b.ST.n), (b.VE, None)])
            ldx = b.LD.n
            for o in range(KD):
                pb = gps[:, (o % 2) * 512:(o % 2) * 512 + 512]
                for k in range(KD):
                    b.mm(pb, wv[:, k * D + o * 128:k * D + (o + 1) * 128],
                         fmv(ka, k, W), start=(k == 0), stop=(k == KD - 1),
                         wait=([(b.SQ, None), (b.VE, None)] if k == 0 else ()),
                         inc=((b.MM, 1) if k == KD - 1 else None))
                b.tt(ALU.mult, fmv(ot, o, W), pb, fmv(rb, o, W),
                     wait=[(b.MM, b.MM.n), (b.LD, ldx)])
                b.tt(ALU.add, fmv(ot, o, W), fmv(ot, o, W), fmv(xc, o, W),
                     inc=(b.VE, 1))
            b.dma("gpsimd", xo[:, c0:c0 + W].rearrange("(k p) t -> p k t", p=128),
                  ot[:].rearrange("p (k t) -> p k t", k=KD), sem=b.ST,
                  wait=[(b.VE, None)])
        b.wait("gpsimd", (b.ST, b.ST.n))


def stage_l2ffn(inp, xs):
    wrt = np.ascontiguousarray(inp["L2_ffn_W_r"].T)
    wkt = np.ascontiguousarray(inp["L2_ffn_W_k"].T)
    wvt = np.ascontiguousarray(inp["L2_ffn_W_v"].T)
    murh = _vec8(inp["L2_ffn_mu_r"])
    mukh = _vec8(inp["L2_ffn_mu_k"])
    maps = []
    for c in range(NCORE):
        bb, h = c // 2, c % 2
        ext = np.zeros((O + 1, D), np.float32)
        ext[1:] = xs[bb, h * O:(h + 1) * O]
        if h == 1:
            ext[0] = xs[bb, O - 1]
        maps.append({
            "xTe": np.ascontiguousarray(ext.T),
            "wrt": wrt, "wkt": wkt, "wvt": wvt, "murh": murh, "mukh": mukh,
        })
    return maps


def l2_ffn(inp, xs):
    maps = stage_l2ffn(inp, xs)
    r = run("l2ffn", build_l2ffn, maps)
    out = np.empty_like(xs)
    for c in range(NCORE):
        bb, h = c // 2, c % 2
        out[bb, h * O:(h + 1) * O] = r[c]["xo"].T
    return out


def stage_l3ffn(inp, xs):
    import ml_dtypes
    gw = np.ascontiguousarray(inp["L3_ffn_gate_w"])
    identh = np.eye(128, dtype=ml_dtypes.bfloat16)
    w1s = np.concatenate([inp["L3_ffn_rt_gate"], inp["L3_ffn_sh_gate"][None]], 0)
    w3s = np.concatenate([inp["L3_ffn_rt_up"], inp["L3_ffn_sh_up"][None]], 0)
    w2s = np.concatenate([inp["L3_ffn_rt_down"], inp["L3_ffn_sh_down"][None]], 0)
    maps = []
    for c in range(NCORE):
        bb, h = c // 2, c % 2
        xonw = xs[bb, h * O:(h + 1) * O]
        maps.append({
            "xT": np.ascontiguousarray(xonw.T), "x_tm": np.ascontiguousarray(xonw),
            "gw": gw, "identh": identh,
            "w1s": np.ascontiguousarray(w1s), "w3s": np.ascontiguousarray(w3s),
            "w2s": np.ascontiguousarray(w2s),
        })
    return maps


def l3_ffn(inp, xs):
    maps = stage_l3ffn(inp, xs)
    r = run("l3ffn", build_l3ffn, maps)
    out = np.empty_like(xs)
    for c in range(NCORE):
        bb, h = c // 2, c % 2
        out[bb, h * O:(h + 1) * O] = r[c]["xo"]
    return out


def barrier(b):
    for eng in Bld.ENGINES:
        b.wait(eng, *[(s, None) for s in b.sems])


def emit_convert(nc, b, src, dst, identfh, tm2fm):
    S = src.shape[0] if tm2fm else src.shape[1]
    with ExitStack() as stack:
        en = stack.enter_context
        tin = en(SB(nc, "cv_in", [128, D], F32))
        tout = en(SB(nc, "cv_out", [128, 2 * D], F32))
        identf = en(SB(nc, "cv_id", [128, 128], F32))
        pcv = en(PS(nc, "cv_ps", [128, 1024], F32))
        b.dma("sync", identf[:], identfh)
        ld_id = b.LD.n
        for s in range(S // 128):
            buf = s % 2
            if tm2fm:
                b.dma("sync", tin[:], src[s * 128:(s + 1) * 128, :],
                      wait=[(b.MM, None), (b.LD, ld_id), (b.SQ, None)])
            else:
                b.dma("sync", tin[:].rearrange("p (k c) -> p k c", k=KD),
                      rearr_in(src[:, s * 128:(s + 1) * 128]),
                      wait=[(b.MM, None), (b.LD, ld_id), (b.SQ, None)])
            ld = b.LD.n
            for k in range(KD):
                b.op("tensor", "transpose", pcv[:, k * 128:(k + 1) * 128],
                     tin[:, k * 128:(k + 1) * 128], identf[:],
                     wait=([(b.LD, ld), (b.VE, None)] if k == 0 else ()),
                     inc=((b.MM, 1) if k == KD - 1 else None))
            b.op("vector", "tensor_copy", tout[:, buf * D:(buf + 1) * D],
                 pcv[:], wait=[(b.MM, b.MM.n), (b.ST, b.ST.n)], inc=(b.VE, 1))
            if tm2fm:
                b.dma("gpsimd",
                      dst[:, s * 128:(s + 1) * 128].rearrange("(k p) c -> p k c", p=128),
                      tout[:, buf * D:(buf + 1) * D].rearrange("p (k c) -> p k c", k=KD),
                      sem=b.ST, wait=[(b.VE, None)])
            else:
                b.dma("gpsimd", dst[s * 128:(s + 1) * 128, :],
                      tout[:, buf * D:(buf + 1) * D], sem=b.ST,
                      wait=[(b.VE, None)])
        b.wait("gpsimd", (b.ST, b.ST.n))


def build_mega():
    nc = bass.Bass("TRN2", target_bir_lowering=False, debug=False)
    S = T
    b = Bld(nc)
    EI = lambda nm, shp, dt=F32: nc.dram_tensor(nm, shp, dt, kind="ExternalInput").ap()
    IN = lambda nm, shp, dt=F32: nc.dram_tensor(nm, shp, dt, kind="Internal").ap()
    xf0 = EI("xf0", [D, S])
    identfh = EI("identfh", [128, 128])
    xo_ext = nc.dram_tensor("xo", [S, D], F32, kind="ExternalOutput").ap()
    z0 = IN("z0s", [D, S]); n0 = IN("n0s", [D, S])
    xf1 = IN("xf1", [D, S]); xt1 = IN("xt1", [S, D])
    xt2 = IN("xt2", [S, D]); xf2 = IN("xf2", [D, S])
    xt3 = IN("xt3", [S, D]); xf3 = IN("xf3", [D, S])
    xf4 = IN("xf4", [D, S]); xt4 = IN("xt4", [S, D])
    xt5 = IN("xt5", [S, D]); xte = IN("xte", [D, S + 1])
    xf6 = IN("xf6", [D, S]); xt6 = IN("xt6", [S, D])
    xt7 = IN("xt7", [S, D]); xf7 = IN("xf7", [D, S])

    def decl(pfx, names_shapes):
        out = {}
        for nm, shp, dt in names_shapes:
            out[nm] = nc.dram_tensor(pfx + nm, shp, dt, kind="ExternalInput").ap()
        return out

    KD8 = [128, KD]
    a_io = decl("a_", [("inw", KD8, F32)])
    b_io = decl("b_", [("alog", KD8, F32), ("cout", KD8, F32),
                       ("dskip", KD8, F32), ("ffw", KD8, F32),
                       ("fcw", [D, 2048], F32), ("pjw", [2048, D], F32)])
    f0_io = decl("f0_", [("gw", [D, 8], F32), ("gbc", [128, 8], F32),
                         ("identh", [128, 128], BF16),
                         ("w1s", [9, D, 2048], F32), ("w3s", [9, D, 2048], F32),
                         ("w2s", [9, 2048, D], F32)])
    a1_names = [("gamh", KD8, F32), ("beth", KD8, F32), ("wq", [128, 128], F32),
                ("wk", [128, 128], F32), ("wv", [128, 128], F32),
                ("wpt", [128, 128], F32), ("bq", [128, 1], F32),
                ("bk", [128, 1], F32), ("bvr", [1, 128], F32),
                ("ub", [128, 8], F32), ("vb", [128, 8], F32), ("wo", [D, D], F32),
                ("bobc", [128, D], F32), ("cosq", [8, 64, S], F32),
                ("sinq", [8, 64, S], F32), ("cos2q", [8, 64, S], F32),
                ("sin2q", [8, 64, S], F32), ("philo", [8, 128, S], F32),
                ("phiup", [8, 128, S], F32), ("mloh", [4, 128, 512], F32),
                ("muph", [4, 128, 512], F32), ("mcr1h", [128, 512], F32),
                ("identh", [128, 128], BF16)]
    a1_io = decl("a1_", a1_names)
    a1_io["zscr"] = IN("a1_zscr", [D, S], BF16)
    a1_io["cscr"] = IN("a1_cscr", [8, 128, S], BF16)
    a1_io["xkv"] = xf2
    a1_io["x_tm"] = xt2
    a1_io["xo"] = xt3
    f1_io = decl("f1_", [("gw", [D, 4], F32), ("gbt4", [128, 64], F32),
                         ("identh", [128, 128], BF16), ("w1h", [4, D, D], F32),
                         ("w2h", [4, D, D], F32), ("colh", [128, 4 * 6 * KD], F32)])
    gqa_names = [("wqh", [D, D], F32), ("wkh", [D, D], F32), ("wvh", [D, D], F32),
                 ("woh", [D, D], F32), ("bqc", KD8, F32), ("bkc", KD8, F32),
                 ("bvc", [128, D], F32), ("qwc", [128, 1], F32),
                 ("kwc", [128, 1], F32), ("hselh", [128, 2], F32),
                 ("hselth", [2, 128], F32), ("ctabq", [128, S], F32),
                 ("stabq", [128, S], F32), ("ctabk", [128, S], F32),
                 ("stabk", [128, S], F32), ("dmaskh", [4, 128, 512], F32),
                 ("identh", [128, 128], BF16)]
    a2_io = decl("a2_", gqa_names)
    a2_io["cscr"] = IN("a2_cscr", [8, 128, S], BF16)
    a2_io["xkv"] = xf4
    a2_io["x_tm"] = xt4
    a2_io["xo"] = xt5
    f2_io = decl("f2_", [("wrt", [D, D], F32), ("wkt", [D, D], F32),
                         ("wvt", [D, D], F32), ("murh", KD8, F32),
                         ("mukh", KD8, F32)])
    a3_io = decl("a3_", gqa_names)
    a3_io["cscr"] = IN("a3_cscr", [8, 128, S], BF16)
    a3_io["xkv"] = xf6
    a3_io["x_tm"] = xt6
    a3_io["xo"] = xt7
    f3_io = decl("f3_", [("gw", [D, 8], F32), ("identh", [128, 128], BF16),
                         ("w1s", [9, D, 2048], F32), ("w3s", [9, D, 2048], F32),
                         ("w2s", [9, 2048, D], F32)])

    halves = ((0, True), (O, False))
    for (h0, _) in halves:
        emit_l0a(nc, b, xf0[:, h0:h0 + O], a_io["inw"], z0[:, h0:h0 + O],
                 n0[:, h0:h0 + O])
        barrier(b)
    for (h0, zp) in halves:
        emit_l0b(nc, b, n0[:, 0:O], n0[:, h0:h0 + O], z0[:, h0:h0 + O],
                 xf0[:, h0:h0 + O], b_io["alog"], b_io["cout"], b_io["dskip"],
                 b_io["ffw"], b_io["fcw"], b_io["pjw"], xf1[:, h0:h0 + O],
                 zero_prefix_carry=zp)
        barrier(b)
    emit_convert(nc, b, xf1, xt1, identfh, tm2fm=False)
    barrier(b)
    for (h0, _) in halves:
        emit_l0ffn(nc, b, xf1[:, h0:h0 + O], xt1[h0:h0 + O, :], f0_io["gw"],
                   f0_io["gbc"], f0_io["identh"], f0_io["w1s"], f0_io["w3s"],
                   f0_io["w2s"], xt2[h0:h0 + O, :])
        barrier(b)
    emit_convert(nc, b, xt2, xf2, identfh, tm2fm=True)
    barrier(b)
    emit_l1attn(nc, b, a1_io)
    barrier(b)
    emit_convert(nc, b, xt3, xf3, identfh, tm2fm=True)
    barrier(b)
    for (h0, _) in halves:
        emit_l1ffn(nc, b, xf3[:, h0:h0 + O], f1_io["gw"], f1_io["gbt4"],
                   f1_io["identh"], f1_io["w1h"], f1_io["w2h"], f1_io["colh"],
                   xf4[:, h0:h0 + O])
        barrier(b)
    emit_convert(nc, b, xf4, xt4, identfh, tm2fm=False)
    barrier(b)
    emit_gqa(nc, b, a2_io, True, True, False, 0)
    barrier(b)
    emit_convert(nc, b, xt5, xte[:, 1:S + 1], identfh, tm2fm=True)
    barrier(b)
    with ExitStack() as stack:
        zc0 = stack.enter_context(SB(nc, "halo0", [128, KD], F32))
        b.op("vector", "memset", zc0[:], 0.0, inc=(b.VE, 1))
        b.dma("gpsimd", xte[:, 0:1].rearrange("(k p) c -> p k c", p=128),
              zc0[:].rearrange("p (k c) -> p k c", c=1),
              sem=b.ST, wait=[(b.VE, None)], allow_slow_non_contiguous=True)
        b.wait("gpsimd", (b.ST, b.ST.n))
    barrier(b)
    for (h0, _) in halves:
        emit_l2ffn(nc, b, xte[:, h0:h0 + O + 1], f2_io["wrt"], f2_io["wkt"],
                   f2_io["wvt"], f2_io["murh"], f2_io["mukh"], xf6[:, h0:h0 + O])
        barrier(b)
    emit_convert(nc, b, xf6, xt6, identfh, tm2fm=False)
    barrier(b)
    emit_gqa(nc, b, a3_io, False, False, True, 128)
    barrier(b)
    emit_convert(nc, b, xt7, xf7, identfh, tm2fm=True)
    barrier(b)
    for (h0, _) in halves:
        emit_l3ffn(nc, b, xf7[:, h0:h0 + O], xt7[h0:h0 + O, :], f3_io["gw"],
                   f3_io["identh"], f3_io["w1s"], f3_io["w3s"], f3_io["w2s"],
                   xo_ext[h0:h0 + O, :])
        barrier(b)
    with ExitStack() as stack:
        b.finalize(stack)
    return nc


def stage_mega(inp):
    xs = np.ascontiguousarray(np.asarray(inp["L0_attn_x"], dtype=np.float32))
    m_a = stage_l0a(inp, xs)
    m_b = stage_l0b(inp, xs, [None] * NCORE, [None] * NCORE)
    m_f0 = stage_l0ffn(inp, xs)
    m_a1 = stage_l1attn(inp, xs)
    m_f1 = stage_l1ffn(inp, xs)
    m_a2 = stage_gqa(inp, xs, 2)
    m_f2 = stage_l2ffn(inp, xs)
    m_a3 = stage_gqa(inp, xs, 3)
    m_f3 = stage_l3ffn(inp, xs)
    identf = np.eye(128, dtype=np.float32)
    maps = []
    for c in range(NCORE):
        bb = c // 2
        m = {"xf0": np.ascontiguousarray(xs[bb].T), "identfh": identf}
        m.update({"a_inw": m_a[c]["inw"]})
        for nm in ("alog", "cout", "dskip", "ffw", "fcw", "pjw"):
            m["b_" + nm] = m_b[c][nm]
        for nm in ("gw", "gbc", "identh", "w1s", "w3s", "w2s"):
            m["f0_" + nm] = m_f0[c][nm]
        for nm in ("gamh", "beth", "wq", "wk", "wv", "wpt", "bq", "bk", "bvr",
                   "ub", "vb", "wo", "bobc", "cosq", "sinq", "cos2q", "sin2q",
                   "philo", "phiup", "mloh", "muph", "mcr1h", "identh"):
            m["a1_" + nm] = m_a1[c][nm]
        for nm in ("gw", "gbt4", "identh", "w1h", "w2h", "colh"):
            m["f1_" + nm] = m_f1[c][nm]
        for nm in ("wqh", "wkh", "wvh", "woh", "bqc", "bkc", "bvc", "qwc",
                   "kwc", "hselh", "hselth", "ctabq", "stabq", "ctabk",
                   "stabk", "dmaskh", "identh"):
            m["a2_" + nm] = m_a2[c][nm]
            m["a3_" + nm] = m_a3[c][nm]
        for nm in ("wrt", "wkt", "wvt", "murh", "mukh"):
            m["f2_" + nm] = m_f2[c][nm]
        for nm in ("gw", "identh", "w1s", "w3s", "w2s"):
            m["f3_" + nm] = m_f3[c][nm]
        maps.append(m)
    return maps


def kernel_mega(inp):
    maps = stage_mega(inp)
    r = run("mega", build_mega, maps)
    out = np.empty((B, T, D), np.float32)
    for bb in range(B):
        out[bb] = r[2 * bb]["xo"]
    return out


def _vec8(w):
    return np.ascontiguousarray(w.reshape(KD, 128).T)


def stage_l0a(inp, xs):
    inw = _vec8(inp["L0_attn_input_norm_w"])
    maps = []
    for c in range(NCORE):
        bb, h = c // 2, c % 2
        xT = np.ascontiguousarray(xs[bb, h * O:(h + 1) * O].T)
        maps.append({"xT": xT, "inw": inw})
    return maps


def stage_l0b(inp, xs, z, n):
    alog = _vec8(inp["L0_attn_a_log"])
    cout = _vec8(inp["L0_attn_c_out"])
    dskip = _vec8(inp["L0_attn_d_skip"])
    ffw = _vec8(inp["L0_attn_ff_norm_w"])
    fcw = np.ascontiguousarray(inp["L0_attn_ff_fc_w"].T)
    pjw = np.ascontiguousarray(inp["L0_attn_ff_proj_w"].T)
    zeros = np.zeros((D, O), np.float32)
    maps = []
    for c in range(NCORE):
        bb, h = c // 2, c % 2
        maps.append({
            "nprev": zeros if h == 0 else n[2 * bb],
            "nown": n[c], "zT": z[c],
            "xT": np.ascontiguousarray(xs[bb, h * O:(h + 1) * O].T),
            "alog": alog, "cout": cout, "dskip": dskip, "ffw": ffw,
            "fcw": fcw, "pjw": pjw,
        })
    return maps


_prog_cache = {}


def prog(name, fn):
    if name not in _prog_cache:
        _prog_cache[name] = fn()
    return _prog_cache[name]


def run(name, fn, maps):
    res = run_bass_kernel_spmd(prog(name, fn), maps, core_ids=list(range(NCORE)))
    return res.results


def l0_attn(inp, xs):
    maps = stage_l0a(inp, xs)
    r = run("l0a", build_l0a, maps)
    z = [r[c]["zT"] for c in range(NCORE)]
    n = [r[c]["nT"] for c in range(NCORE)]
    maps = stage_l0b(inp, xs, z, n)
    r = run("l0b", build_l0b, maps)
    out = np.empty_like(xs)
    for c in range(NCORE):
        bb, h = c // 2, c % 2
        out[bb, h * O:(h + 1) * O] = r[c]["xo"].T
    return out


def kernel(**inp):
    return kernel_mega(inp)
